# version 02
# baseline (speedup 1.0000x reference)
import math
import numpy as np
import ml_dtypes
from collections import defaultdict
import concourse.bass as bass
import concourse.mybir as mybir
from concourse.bass_utils import run_bass_kernel_spmd

F32 = mybir.dt.float32
BF16 = mybir.dt.bfloat16
AF = mybir.ActivationFunctionType
ALU = mybir.AluOpType
AX = mybir.AxisListType
BFNP = ml_dtypes.bfloat16

L = 2048
D = 1024
NCD = 8
H = 16
NHB = 32
NT = 4
NKT = 16
NSUP = 32

PHASES = ["attn_a", "ffn0", "titans", "ffn1"]
DEBUG_DUMP = [None]
DEBUG_BARRIER = [False]
FFN_SKIP_B = [False]
TIT_ONLY = [None]
TIT_STEPS = [None]
TIT_CUT = [None]
TIT_HEADS = [None]


class Plan:
    def __init__(self):
        self.items = []
        self.cnt = defaultdict(int)

    def op(self, eng, fn, inc=None):
        if inc is not None:
            self.cnt[inc[0]] += inc[1]
        self.items.append((eng, "op", (fn, inc)))
        return self.cnt[inc[0]] if inc else None

    def wait(self, eng, sem, val=None):
        v = self.cnt[sem] if val is None else val
        if v > 0:
            self.items.append((eng, "wait", (sem, v)))
        return v

    def drain(self, eng):
        self.items.append((eng, "op", (lambda e: e.drain(), None)))

    def val(self, sem):
        return self.cnt[sem]

    def replay(self, eng_name, eng_obj, sems):
        for eng, kind, payload in self.items:
            if eng != eng_name:
                continue
            if kind == "op":
                fn, inc = payload
                ins = fn(eng_obj)
                if inc is not None:
                    ins.then_inc(sems[inc[0]], inc[1])
            else:
                sem, v = payload
                eng_obj.wait_ge(sems[sem], v)


def _stage_sq(w):
    return w.reshape(8, 128, 1024).transpose(1, 0, 2).reshape(128, 8192)


def _build_consts():
    cols = {}
    parts = []
    c = [0]

    def add(name, arr):
        cols[name] = c[0]
        parts.append(arr.astype(np.float32))
        c[0] += arr.shape[1]

    add("ident", np.eye(128))
    add("ones1", np.ones((128, 1)))
    o = np.zeros((128, 128)); o[0, :] = 1; o[64, :] = 1
    add("onesT", o)
    m2 = np.zeros((128, 2)); m2[0:64, 0] = 1; m2[64:128, 1] = 1
    add("mask2", m2)
    s2 = np.zeros((128, 128)); s2[0, 0:64] = 1; s2[1, 64:128] = 1
    add("sel2T", s2)
    m16 = np.zeros((128, 8 * 16))
    for ch in range(8):
        m16[0:64, ch * 16 + 2 * ch] = 1
        m16[64:128, ch * 16 + 2 * ch + 1] = 1
    add("m16", m16)
    bm = np.zeros((64, 64))
    for j in range(64):
        for i in range(64):
            if (j // 16 < i // 16) or (j // 16 == i // 16 and j <= i):
                bm[j, i] = 1
    bm2 = np.concatenate([bm, bm], axis=0)
    add("bmask8", bm2)
    import math as _m
    add("eps_ln", np.full((128, 1), 1e-5))
    add("eps_l2", np.full((128, 1), 1e-12))
    add("eps_q", np.full((128, 1), 1e-30))
    add("eps_gn", np.full((128, 1), 64e-5))
    add("ln8", np.full((128, 1), _m.log(8.0)))
    return np.ascontiguousarray(np.concatenate(parts, axis=1)), cols


CF_ARR, CC = _build_consts()


def stage_inputs(inp):
    g = {}
    X = inp["L0_attn_a_x"]
    xt = X.transpose(0, 2, 1)
    g["x_cores"] = [
        np.ascontiguousarray(
            xt[2 * i:2 * i + 2].reshape(2, 8, 128, L).transpose(0, 2, 1, 3)
            .reshape(2, 128, 8 * L))
        for i in range(8)
    ]
    g["aw"] = np.ascontiguousarray(np.stack([
        _stage_sq(inp["L0_attn_a_Wq"]), _stage_sq(inp["L0_attn_a_Wk"]),
        _stage_sq(inp["L0_attn_a_Wv"]), _stage_sq(inp["L0_attn_a_Wo"])]))
    g["bw"] = np.ascontiguousarray(np.stack([
        _stage_sq(np.ascontiguousarray(inp["L0_attn_b_Wq"].T)),
        _stage_sq(np.ascontiguousarray(inp["L0_attn_b_Wk"].T)),
        _stage_sq(np.ascontiguousarray(inp["L0_attn_b_Wv"].T)),
        _stage_sq(np.ascontiguousarray(inp["L0_attn_b_Wg"].T)),
        _stage_sq(np.ascontiguousarray(inp["L0_attn_b_Wo"].T))]))
    WU = np.concatenate([inp["L0_ffn_Wu"], inp["L0_ffn_Wu_s"][None]], 0)
    WG = np.concatenate([inp["L0_ffn_Wg"], inp["L0_ffn_Wg_s"][None]], 0)
    WD = np.concatenate([inp["L0_ffn_Wd"], inp["L0_ffn_Wd_s"][None]], 0)
    u5 = WU.reshape(8, 8, 128, 32, 128).transpose(0, 3, 2, 1, 4)
    g5 = WG.reshape(8, 8, 128, 32, 128).transpose(0, 3, 2, 1, 4)
    g["fwug"] = np.ascontiguousarray(
        np.concatenate([u5.reshape(8, 32, 128, 1024),
                        g5.reshape(8, 32, 128, 1024)], axis=3))
    g["fwd"] = np.ascontiguousarray(
        WD.reshape(8, 32, 128, 8, 128).transpose(0, 3, 2, 1, 4)
        .reshape(8, 8, 128, 4096))
    g["wr"] = np.ascontiguousarray(
        inp["L0_ffn_Wr"].reshape(8, 128, 8).transpose(1, 0, 2).reshape(128, 64))
    g["bo"] = np.ascontiguousarray(inp["L0_attn_a_bo"].reshape(8, 128).T)
    cw = np.stack([inp["L0_attn_b_conv_q"], inp["L0_attn_b_conv_k"],
                   inp["L0_attn_b_conv_v"]])
    g["convw"] = np.ascontiguousarray(
        cw.reshape(3, 8, 128, 4).transpose(2, 0, 1, 3).reshape(128, 96))
    w0 = np.zeros((128, 512), dtype=np.float32)
    wi = inp["L0_attn_b_W_init"][0]
    for h in range(16):
        w0[(h % 2) * 64:(h % 2) * 64 + 64,
           (h // 2) * 64:(h // 2) * 64 + 64] = wi[h]
    g["w0"] = w0
    gb = np.zeros((128, 2), dtype=np.float32)
    gb[:, 0] = np.tile(inp["L0_attn_b_ln_gamma"], 2)
    gb[:, 1] = np.tile(inp["L0_attn_b_ln_beta"], 2)
    g["gb"] = gb
    g["constf"] = CF_ARR
    cb = np.concatenate([np.eye(128, dtype=BFNP),
                         np.ones((128, 1), dtype=BFNP)], axis=1)
    g["constb"] = np.ascontiguousarray(cb)
    return g


def build_program():
    nc = bass.Bass("TRN2", target_bir_lowering=False,
                   dynamic_dma_scratch_size=16384)
    P = Plan()

    x_h = nc.dram_tensor("x", [2, 128, NCD * L], F32, kind="ExternalInput")
    y_h = nc.dram_tensor("y", [2, 128, NCD * L], F32, kind="ExternalOutput")
    aw_h = nc.dram_tensor("aw", [4, 128, 8192], F32, kind="ExternalInput")
    bw_h = nc.dram_tensor("bw", [5, 128, 8192], F32, kind="ExternalInput")
    fwug_h = nc.dram_tensor("fwug", [8, 32, 128, 2048], F32, kind="ExternalInput")
    fwd_h = nc.dram_tensor("fwd", [8, 8, 128, 4096], F32, kind="ExternalInput")
    wr_h = nc.dram_tensor("wr", [128, 64], F32, kind="ExternalInput")
    bo_h = nc.dram_tensor("bo", [128, 8], F32, kind="ExternalInput")
    conv_h = nc.dram_tensor("convw", [128, 96], F32, kind="ExternalInput")
    w0_h = nc.dram_tensor("w0", [128, 512], F32, kind="ExternalInput")
    gb_h = nc.dram_tensor("gb", [128, 2], F32, kind="ExternalInput")
    cf_h = nc.dram_tensor("constf", [128, CF_ARR.shape[1]], F32,
                          kind="ExternalInput")
    cb_h = nc.dram_tensor("constb", [128, 129], BF16, kind="ExternalInput")
    park_h = nc.dram_tensor("park", [128, NCD * L], F32, kind="Internal")

    BASE = 16384 + 32
    off = [BASE]

    def at(name, shape, dtype, addr=None):
        nbytes = int(np.prod(shape[1:])) * mybir.dt.size(dtype)
        if addr is None:
            addr = off[0]
            off[0] = addr + (nbytes + 31) // 32 * 32
        return nc.alloc_sbuf_tensor_at(name, list(shape), dtype, offset=addr)

    cf_sb = at("cf_sb", [128, CF_ARR.shape[1]], F32)
    cb_sb = at("cb_sb", [128, 129], BF16)
    wr_sb = at("wr_sb", [128, 64], BF16)
    bo_sb = at("bo_sb", [128, 8], F32)
    conv_sb = at("conv_sb", [128, 96], F32)
    w3p_sb = at("w3p_sb", [128, 24], F32)
    gb_sb = at("gb_sb", [128, 2], F32)
    wsb = at("wsb", [128, 512], BF16)
    scr = at("scr", [128, 2048], F32)
    trow = at("trow", [128, 512], F32)
    SCR_ADDR = nc.lookup_mloc(scr).addr
    qkt_sb = at("qkt_sb", [128, 1024], BF16, SCR_ADDR + 1536 * 4)
    w0_sb = at("w0_sb", [128, 512], F32, SCR_ADDR + 1536 * 4)

    X0 = off[0]
    x_sb = at("x_sb", [128, NCD * L], F32, X0)
    XN0 = X0 + NCD * L * 4
    xn_sb = at("xn_sb", [128, NCD * L], BF16, XN0)
    A0 = XN0 + NCD * L * 2
    bufA = at("bufA", [128, NCD * L], BF16, A0)
    B0 = A0 + NCD * L * 2
    bufB = at("bufB", [128, NCD * L], BF16, B0)
    C0 = B0 + NCD * L * 2
    bufC = at("bufC", [128, 16 * 1040], BF16, C0)
    CEND = C0 + 16 * 1040 * 2
    assert CEND - 16384 <= 224 * 1024 - 64, f"sbuf overflow {CEND}"

    wres = at("wres", [128, 8192], BF16, X0)
    wres2 = at("wres2", [128, 8192], BF16, X0 + 16384)
    qn_sb = at("qn_sb", [64, 2048], F32, X0 + 32768)
    kn_sb = at("kn_sb", [64, 2048], F32, X0 + 40960)
    bb_sb = at("bb_sb", [64, 2048], F32, X0 + 49152)
    mk_sb = at("mk_sb", [64, 32], F32, X0 + 57344)
    pb_sb = at("pb_sb", [128, 3 * 512], BF16, X0 + 57600)
    ktm_sb = at("ktm_sb", [128, 16 * 1024], BF16, X0)
    gate_sb = at("gate_sb", [128, NCD * L], BF16, X0 + 32768)
    ofm_sb = at("ofm_sb", [128, NCD * L], BF16, XN0)
    act_sb = at("act_sb", [128, NHB * 1024], BF16, A0)
    wresA = at("wresA", [128, 8192], BF16, A0)
    wresC = at("wresC", [128, 8192], BF16, C0)
    wffa_t = at("wffa", [128, 2 * 2048], BF16, C0)
    wffd_t = at("wffd", [128, 2 * 4096], BF16, C0 + 8192)
    wffa = [wffa_t[:, 0:2048], wffa_t[:, 2048:4096]]
    wffd = [wffd_t[:, 0:4096], wffd_t[:, 4096:8192]]
    wb_sb = at("wb_sb", [128, L], BF16, C0 + 24576)
    wrow_sb = at("wrow_sb", [8, L], BF16, C0 + 28672)
    vtm_t = at("vtm_t", [128, 16 * 1024], BF16, C0)

    ps = nc.alloc_psum_tensor("ps", [128, 3584], F32)
    psb = nc.alloc_psum_tensor("psb", [128, 1024], BF16)

    sem_names = ["ci", "cw", "xi", "wa", "wb", "wx",
                 "f0", "f1", "g0", "g1", "pe", "dv", "ac", "out"]

    def bank(i, w=512):
        return ps[:, i * 512: i * 512 + w]

    IDF = cf_sb[:, CC["ident"]:CC["ident"] + 128]
    ONES1 = cf_sb[:, CC["ones1"]:CC["ones1"] + 1]
    ONEST = cf_sb[0:1, CC["onesT"]:CC["onesT"] + 128]
    ONEST64 = cf_sb[64:65, CC["onesT"]:CC["onesT"] + 128]
    MASK2 = cf_sb[:, CC["mask2"]:CC["mask2"] + 2]
    SEL2T = cf_sb[0:2, CC["sel2T"]:CC["sel2T"] + 128]
    BMASK8 = cf_sb[:, CC["bmask8"]:CC["bmask8"] + 64]
    IDB = cb_sb[:, 0:128]
    ONESB = cb_sb[:, 128:129]

    def m16(ch):
        return cf_sb[:, CC["m16"] + ch * 16: CC["m16"] + ch * 16 + 16]

    def xv(dc, t0, n):
        return x_sb[:, dc * L + t0: dc * L + t0 + n]

    def xnv(dc, t0, n):
        return xn_sb[:, dc * L + t0: dc * L + t0 + n]

    def fmv(buf, ch, t0, n):
        return buf[:, ch * L + t0: ch * L + t0 + n]

    P.op("sync", lambda s: s.dma_start(out=cf_sb[:], in_=cf_h.ap()), inc=("ci", 16))
    P.op("sync", lambda s: s.dma_start(out=bo_sb[:], in_=bo_h.ap()), inc=("ci", 16))
    P.op("sync", lambda s: s.dma_start(out=conv_sb[:], in_=conv_h.ap()), inc=("ci", 16))
    P.op("sync", lambda s: s.dma_start(out=gb_sb[:], in_=gb_h.ap()), inc=("ci", 16))
    P.op("gpsimd", lambda g: g.dma_start(out=cb_sb[:], in_=cb_h.ap()), inc=("cw", 16))
    P.op("gpsimd", lambda g: g.dma_start(out=wr_sb[:], in_=wr_h.ap()), inc=("cw", 16))
    INIT_CI = P.val("ci")
    INIT_CW = P.val("cw")
    P.wait("scalar", "ci", INIT_CI)
    P.wait("vector", "ci", INIT_CI)
    P.wait("tensor", "ci", INIT_CI)
    P.wait("tensor", "cw", INIT_CW)
    P.op("vector", lambda v: v.tensor_scalar(
        w3p_sb[:],
        conv_sb[:].rearrange("p (a k) -> p a k", k=4)[:, :, 3],
        1.0, None, ALU.add), inc=("dv", 1))
    P.drain("vector")

    def emit_ln(x_ready_xd, x_ready_dv):
        P.wait("tensor", "xi", x_ready_xd)
        P.wait("tensor", "dv", x_ready_dv)
        P.wait("scalar", "xi", x_ready_xd)
        P.wait("scalar", "dv", x_ready_dv)
        P.wait("vector", "xi", x_ready_xd)
        for T in range(NT):
            t0 = T * 512
            sum_ps = ps[0:1, 0:512]
            sq_ps = ps[0:1, 512:1024]
            sq_done = {}
            for dc in range(NCD):
                s = scr[:, (dc % 2) * 512:(dc % 2) * 512 + 512]
                if dc >= 2:
                    P.wait("scalar", "pe", sq_done[dc - 2])
                elif T > 0:
                    P.wait("scalar", "pe")
                P.op("scalar", lambda sc, s=s, dc=dc, t0=t0:
                     sc.activation(s, xv(dc, t0, 512), AF.Square), inc=("ac", 1))
                P.wait("tensor", "ac")
                P.op("tensor", lambda t, dc=dc, t0=t0, sum_ps=sum_ps:
                     t.matmul(sum_ps, ONES1, xv(dc, t0, 512),
                              start=(dc == 0), stop=(dc == NCD - 1)))
                sq_done[dc] = P.op("tensor", lambda t, s=s, dc=dc, sq_ps=sq_ps:
                                   t.matmul(sq_ps, ONES1, s,
                                            start=(dc == 0), stop=(dc == NCD - 1)),
                                   inc=("pe", 1))
            m_r = trow[0:1, 0:512]
            v_r = scr[0:1, 1024:1536]
            mr_r = scr[0:1, 0:512]
            P.wait("scalar", "pe")
            P.wait("scalar", "dv")
            P.op("scalar", lambda sc, m_r=m_r, sum_ps=sum_ps:
                 sc.activation(m_r, sum_ps, AF.Copy, scale=1.0 / D))
            P.drain("scalar")
            P.op("scalar", lambda sc, v_r=v_r, sq_ps=sq_ps:
                 sc.activation(v_r, sq_ps, AF.Copy, scale=1.0 / D), inc=("ac", 1))
            P.drain("scalar")
            P.wait("vector", "ac")
            t1 = scr[0:1, 0:512]
            P.op("vector", lambda v, t1=t1, m_r=m_r: v.tensor_mul(t1, m_r, m_r))
            P.drain("vector")
            P.op("vector", lambda v, t1=t1, v_r=v_r: v.tensor_sub(v_r, v_r, t1),
                 inc=("dv", 1))
            P.drain("vector")
            P.wait("scalar", "dv")
            P.op("scalar", lambda sc, v_r=v_r:
                 sc.activation(v_r, v_r, AF.Ln,
               bias=cf_sb[0:1, CC["eps_ln"]:CC["eps_ln"] + 1]))
            P.drain("scalar")
            P.op("scalar", lambda sc, v_r=v_r:
                 sc.activation(v_r, v_r, AF.Exp, scale=-0.5), inc=("ac", 1))
            P.drain("scalar")
            P.wait("vector", "ac")
            P.op("vector", lambda v, mr_r=mr_r, m_r=m_r, v_r=v_r:
                 v.tensor_mul(mr_r, m_r, v_r), inc=("dv", 1))
            P.drain("vector")
            rb = bank(2)
            mrb = bank(3)
            P.wait("tensor", "dv")
            P.op("tensor", lambda t, v_r=v_r, rb=rb:
                 t.matmul(rb, ONEST, v_r, start=True, stop=True))
            P.op("tensor", lambda t, mr_r=mr_r, mrb=mrb:
                 t.matmul(mrb, ONEST, mr_r, start=True, stop=True), inc=("pe", 1))
            P.wait("vector", "pe")
            for dc in range(NCD):
                u = scr[:, 1024 + (dc % 2) * 512: 1024 + (dc % 2) * 512 + 512]
                P.op("vector", lambda v, u=u, dc=dc, t0=t0, rb=rb:
                     v.tensor_mul(u, xv(dc, t0, 512), rb))
                P.drain("vector")
                P.op("vector", lambda v, u=u, dc=dc, t0=t0, mrb=mrb:
                     v.tensor_sub(xnv(dc, t0, 512), u, mrb), inc=("dv", 1))
                P.drain("vector")
        return P.val("dv")

    def emit_park(dep_dv):
        P.wait("sync", "dv", dep_dv)
        P.op("sync", lambda s: s.dma_start(out=park_h.ap(), in_=x_sb[:]),
             inc=("xi", 16))
        return P.val("xi")

    def emit_unpark(dep_dv):
        P.wait("sync", "dv", dep_dv)
        P.wait("sync", "pe")
        P.wait("sync", "ac")
        P.wait("sync", "xi")
        P.op("sync", lambda s: s.dma_start(out=x_sb[:], in_=park_h.ap()),
             inc=("xi", 16))
        return P.val("xi")

    def emit_attn_a(x_xd, x_dv):
        xn_ready = emit_ln(x_xd, x_dv)
        park_done = emit_park(xn_ready)

        P.wait("gpsimd", "xi", park_done)
        P.op("gpsimd", lambda g: g.dma_start(out=wres[:], in_=aw_h.ap()[0]),
             inc=("wa", 16))
        wq_ld = ("wa", P.val("wa"))

        def proj_fm(dst, wsbuf, wld):
            evac = {}
            n = 0
            for qc in range(NCD):
                for T in range(NT):
                    pb = bank(n % 2)
                    P.wait("tensor", wld[0], wld[1])
                    P.wait("tensor", "dv", xn_ready)
                    if n >= 2:
                        P.wait("tensor", "ac", evac[n - 2])
                    for dc in range(NCD):
                        P.op("tensor", lambda t, qc=qc, T=T, dc=dc, pb=pb, w=wsbuf:
                             t.matmul(pb,
                                      w[:, dc * 1024 + qc * 128:
                                        dc * 1024 + qc * 128 + 128],
                                      xnv(dc, T * 512, 512),
                                      start=(dc == 0), stop=(dc == NCD - 1)),
                             inc=("pe", 1) if dc == NCD - 1 else None)
                    P.wait("scalar", "pe")
                    evac[n] = P.op("scalar", lambda sc, qc=qc, T=T, pb=pb, dst=dst:
                                   sc.activation(fmv(dst, qc, T * 512, 512), pb,
                                                 AF.Copy), inc=("ac", 1))
                    n += 1
            return P.val("ac")

        q_done = proj_fm(bufA, wres, wq_ld)
        P.op("gpsimd", lambda g: g.dma_start(out=wres2[:], in_=aw_h.ap()[1]),
             inc=("wb", 16))
        wk_ld = ("wb", P.val("wb"))
        k_done = proj_fm(bufB, wres2, wk_ld)
        P.wait("gpsimd", "ac", q_done)
        P.op("gpsimd", lambda g: g.dma_start(out=wres[:], in_=aw_h.ap()[2]),
             inc=("wa", 16))
        wv_ld = ("wa", P.val("wa"))

        P.op("vector", lambda v: v.memset(
            bufC[:].rearrange("p (k h c) -> p k h c", h=16, c=65)[:, :, :, 64],
            1.0), inc=("dv", 1))
        vdone = {}
        for tt in range(NKT):
            pb = ps[:, 1024 + (tt % 2) * 1024: 1024 + (tt % 2) * 1024 + 1024]
            P.wait("tensor", wv_ld[0], wv_ld[1])
            if tt >= 2:
                P.wait("tensor", "dv", vdone[tt - 2])
            for dc in range(NCD):
                for hf in range(2):
                    P.op("tensor", lambda t, tt=tt, dc=dc, hf=hf, pb=pb:
                         t.matmul(pb[:, hf * 512:hf * 512 + 512],
                                  xnv(dc, tt * 128, 128),
                                  wres[:, dc * 1024 + hf * 512:
                                       dc * 1024 + hf * 512 + 512],
                                  start=(dc == 0), stop=(dc == NCD - 1)),
                         inc=("pe", 1) if (dc == NCD - 1 and hf == 1) else None)
            P.wait("vector", "pe")
            vdone[tt] = P.op("vector", lambda v, tt=tt, pb=pb:
                 v.tensor_copy(
                     bufC[:].rearrange("p (k h c) -> p k h c", h=16, c=65)
                     [:, tt, :, 0:64],
                     pb.rearrange("p (h c) -> p h c", c=64)), inc=("dv", 1))
        v_done = P.val("dv")

        def norms(src, dst, src_ac):
            for T in range(NT):
                ssq = bank(0) if T % 2 == 0 else bank(1)
                sq_done = {}
                for ch in range(NCD):
                    s = scr[:, (ch % 2) * 512:(ch % 2) * 512 + 512]
                    P.wait("scalar", "ac", src_ac)
                    if ch >= 2:
                        P.wait("scalar", "pe", sq_done[ch - 2])
                    P.op("scalar", lambda sc, ch=ch, T=T, s=s, src=src:
                         sc.activation(s, fmv(src, ch, T * 512, 512), AF.Square),
                         inc=("ac", 1))
                    P.wait("tensor", "ac")
                    sq_done[ch] = P.op("tensor", lambda t, ch=ch, s=s, ssq=ssq:
                         t.matmul(ssq[0:16, :], m16(ch), s,
                                  start=(ch == 0), stop=(ch == NCD - 1)),
                         inc=("pe", 1))
                P.wait("scalar", "pe")
                d = dst[0:16, T * 512:T * 512 + 512]
                P.op("scalar", lambda sc, d=d, ssq=ssq:
                     sc.activation(d, ssq[0:16, :], AF.Ln,
               bias=cf_sb[0:16, CC["eps_q"]:CC["eps_q"] + 1]))
                P.drain("scalar")
                P.op("scalar", lambda sc, d=d:
                     sc.activation(d, d, AF.Exp, scale=0.5), inc=("ac", 1))
                P.drain("scalar")
            return P.val("ac")

        qn_done = norms(bufA, qn_sb, q_done)
        kn_done = norms(bufB, kn_sb, k_done)
        P.wait("vector", "ac", kn_done)
        P.op("vector", lambda v: v.reduce_max(mk_sb[0:16, 0:1], kn_sb[0:16, :],
                                              axis=AX.X))
        P.drain("vector")
        P.op("vector", lambda v: v.tensor_scalar(
            bb_sb[0:16, :], qn_sb[0:16, :], mk_sb[0:16, 0:1], None, ALU.mult),
            inc=("dv", 1))
        P.drain("vector")
        bb_done = P.val("dv")


        for h in range(H):
            ch, par = h // 2, h % 2
            for qt in range(NT):
                pb_av = bank(2 + (h * NT + qt) % 2)
                P.wait("tensor", "dv", bb_done)
                P.wait("tensor", "dv")
                P.wait("tensor", "ac")
                P.op("tensor", lambda t, h=h, qt=qt:
                     t.matmul(ps[0:1, 2560:3072],
                              cf_sb[0:16, CC["ident"] + h:CC["ident"] + h + 1],
                              bb_sb[0:16, qt * 512:qt * 512 + 512],
                              start=True, stop=True), inc=("pe", 1))
                P.wait("scalar", "pe")
                P.op("scalar", lambda sc: sc.activation(
                    trow[0:1, 0:512], ps[0:1, 2560:3072], AF.Copy), inc=("ac", 1))
                P.wait("tensor", "ac")
                P.op("tensor", lambda t: t.matmul(
                    bank(6), ONEST, trow[0:1, 0:512], start=True, stop=True),
                    inc=("pe", 1))
                P.wait("scalar", "pe")
                P.op("scalar", lambda sc: sc.activation(
                    scr[:, 0:512], bank(6), AF.Copy), inc=("ac", 1))
                bbc_ready = P.val("ac")
                sub_done, exp_done, av_done = {}, {}, {}
                for kt in range(NKT):
                    st = bank(kt % 2)
                    pslot = pb_sb[:, (kt % 3) * 512:(kt % 3) * 512 + 512]
                    P.wait("tensor", "ac", bbc_ready)
                    if kt >= 2:
                        P.wait("tensor", "dv", sub_done[kt - 2])
                    P.op("tensor", lambda t, ch=ch, par=par, kt=kt, qt=qt, st=st:
                         t.matmul(st,
                                  fmv(bufB, ch, kt * 128, 128)
                                  [par * 64:par * 64 + 64, :],
                                  fmv(bufA, ch, qt * 512, 512)
                                  [par * 64:par * 64 + 64, :],
                                  start=True, stop=True), inc=("pe", 1))
                    P.wait("vector", "pe")
                    e = scr[:, 512 + (kt % 2) * 512: 512 + (kt % 2) * 512 + 512]
                    if kt >= 2:
                        P.wait("vector", "ac", exp_done[kt - 2])
                    sub_done[kt] = P.op("vector", lambda v, e=e, st=st:
                         v.tensor_sub(e, st, scr[:, 0:512]), inc=("dv", 1))
                    P.wait("scalar", "dv")
                    if kt >= 3:
                        P.wait("scalar", "pe", av_done[kt - 3])
                    exp_done[kt] = P.op("scalar", lambda sc, e=e, pslot=pslot:
                         sc.activation(pslot, e, AF.Exp, scale=0.125),
                         inc=("ac", 1))
                    P.wait("tensor", "ac")
                    P.wait("tensor", "dv", v_done)
                    if par == 0:
                        av_done[kt] = P.op(
                            "tensor", lambda t, kt=kt, h=h, pslot=pslot, pb_av=pb_av:
                            t.matmul(pb_av[0:65, :],
                                     bufC[:, kt * 1040 + h * 65:
                                          kt * 1040 + h * 65 + 65],
                                     pslot, start=(kt == 0), stop=(kt == NKT - 1)),
                            inc=("pe", 1))
                    else:
                        P.op("tensor", lambda t, kt=kt, h=h, pslot=pslot, pb_av=pb_av:
                             t.matmul(pb_av[64:128, :],
                                      bufC[:, kt * 1040 + h * 65:
                                           kt * 1040 + h * 65 + 64],
                                      pslot, start=(kt == 0), stop=(kt == NKT - 1)))
                        av_done[kt] = P.op("tensor", lambda t, kt=kt, pslot=pslot:
                             t.matmul(ps[0:1, 2048:2560], ONESB, pslot,
                                      start=(kt == 0), stop=(kt == NKT - 1)),
                             inc=("pe", 1))
                P.wait("vector", "pe")
                if par == 0:
                    P.op("vector", lambda v, pb_av=pb_av:
                         v.reciprocal(trow[64:65, 0:512], pb_av[64:65, :]),
                         inc=("dv", 1))
                    P.wait("tensor", "dv")
                    P.op("tensor", lambda t: t.matmul(
                        ps[0:64, 2560:3072], ONEST64[:, 0:64], trow[64:65, 0:512],
                        start=True, stop=True), inc=("pe", 1))
                    P.wait("scalar", "pe")
                    P.op("scalar", lambda sc: sc.activation(
                        scr[0:64, 1536:2048], ps[0:64, 2560:3072], AF.Copy),
                        inc=("ac", 1))
                    P.wait("vector", "ac")
                    P.op("vector", lambda v, ch=ch, qt=qt, pb_av=pb_av:
                         v.tensor_mul(fmv(ofm_sb, ch, qt * 512, 512)[0:64, :],
                                      pb_av[0:64, :], scr[0:64, 1536:2048]),
                         inc=("dv", 1))
                else:
                    P.op("vector", lambda v:
                         v.reciprocal(trow[0:1, 0:512], ps[0:1, 2048:2560]),
                         inc=("dv", 1))
                    P.wait("tensor", "dv")
                    P.op("tensor", lambda t: t.matmul(
                        ps[64:128, 2560:3072], ONEST[:, 0:64], trow[0:1, 0:512],
                        start=True, stop=True), inc=("pe", 1))
                    P.wait("scalar", "pe")
                    P.op("scalar", lambda sc: sc.activation(
                        scr[64:128, 1536:2048], ps[64:128, 2560:3072], AF.Copy),
                        inc=("ac", 1))
                    P.wait("vector", "ac")
                    P.op("vector", lambda v, ch=ch, qt=qt, pb_av=pb_av:
                         v.tensor_mul(fmv(ofm_sb, ch, qt * 512, 512)[64:128, :],
                                      pb_av[64:128, :], scr[64:128, 1536:2048]),
                         inc=("dv", 1))
        o_done = P.val("dv")

        P.wait("gpsimd", "pe")
        P.op("gpsimd", lambda g: g.dma_start(out=wresC[:], in_=aw_h.ap()[3]),
             inc=("wx", 16))
        wo_ld = P.val("wx")
        unpark_done = emit_unpark(o_done)

        P.wait("tensor", "xi", unpark_done)
        P.wait("tensor", "wx", wo_ld)
        P.wait("tensor", "dv", o_done)
        prev = {}
        n = 0
        for dc in range(NCD):
            for T in range(NT):
                pb = bank(n % 2)
                if n >= 2:
                    P.wait("tensor", "dv", prev[n - 2])
                for fc in range(NCD):
                    P.op("tensor", lambda t, dc=dc, T=T, fc=fc, pb=pb:
                         t.matmul(pb,
                                  wresC[:, fc * 1024 + dc * 128:
                                        fc * 1024 + dc * 128 + 128],
                                  fmv(ofm_sb, fc, T * 512, 512),
                                  start=(fc == 0), stop=(fc == NCD - 1)),
                         inc=("pe", 1) if fc == NCD - 1 else None)
                P.wait("scalar", "pe")
                s = scr[:, (n % 2) * 512:(n % 2) * 512 + 512]
                P.op("scalar", lambda sc, dc=dc, pb=pb, s=s:
                     sc.activation(s, pb, AF.Identity, bias=bo_sb[:, dc:dc + 1]),
                     inc=("ac", 1))
                P.wait("vector", "ac")
                prev[n] = P.op("vector", lambda v, dc=dc, T=T, s=s:
                     v.tensor_add(xv(dc, T * 512, 512), xv(dc, T * 512, 512), s),
                     inc=("dv", 1))
                n += 1
        P.drain("vector")
        return P.val("dv")

    def emit_ffn(x_xd, x_dv, experts=tuple(range(8))):
        P.wait("gpsimd", "pe")
        P.wait("gpsimd", "dv")
        xn_ready = emit_ln(x_xd, x_dv)

        lg = ps[:, 2048:2048 + 128]
        P.wait("tensor", "cw", INIT_CW)
        P.wait("tensor", "dv", xn_ready)
        for tt in range(NKT):
            for dc in range(NCD):
                P.op("tensor", lambda t, tt=tt, dc=dc:
                     t.matmul(lg[:, tt * 8:tt * 8 + 8],
                              xnv(dc, tt * 128, 128),
                              wr_sb[:, dc * 8:dc * 8 + 8],
                              start=(tt == 0 and dc == 0),
                              stop=(tt == NKT - 1 and dc == NCD - 1)),
                     inc=(("pe", 1) if (tt == NKT - 1 and dc == NCD - 1)
                          else None))
        P.wait("vector", "pe")
        P.op("vector", lambda v: v.tensor_copy(scr[:, 0:128], lg))
        P.drain("vector")
        lgv = scr[:, 0:128].rearrange("p (t e) -> p t e", e=8)
        P.op("vector", lambda v, lgv=lgv:
             v.reduce_max(scr[:, 128:144], lgv, axis=AX.X))
        P.drain("vector")
        for tt in range(NKT):
            P.op("vector", lambda v, tt=tt: v.tensor_scalar(
                scr[:, 256 + tt * 8:256 + tt * 8 + 8],
                scr[:, tt * 8:tt * 8 + 8],
                scr[:, 128 + tt:128 + tt + 1], None, ALU.subtract))
        P.drain("vector")
        P.op("vector", lambda v: v.nop(), inc=("dv", 1))
        P.wait("scalar", "dv")
        P.op("scalar", lambda sc: sc.activation(
            scr[:, 384:512], scr[:, 256:384], AF.Exp), inc=("ac", 1))
        P.wait("vector", "ac")
        ev = scr[:, 384:512].rearrange("p (t e) -> p t e", e=8)
        P.op("vector", lambda v, ev=ev:
             v.reduce_sum(scr[:, 144:160], ev, axis=AX.X))
        P.drain("vector")
        P.op("vector", lambda v: v.reciprocal(scr[:, 160:176], scr[:, 144:160]))
        P.drain("vector")
        for tt in range(NKT):
            P.op("vector", lambda v, tt=tt: v.tensor_scalar(
                scr[:, 256 + tt * 8:256 + tt * 8 + 8],
                scr[:, tt * 8:tt * 8 + 8],
                scr[:, 128 + tt:128 + tt + 1], None, ALU.is_equal))
        P.drain("vector")
        for tt in range(NKT):
            P.op("vector", lambda v, tt=tt: v.tensor_scalar(
                scr[:, 256 + tt * 8:256 + tt * 8 + 8],
                scr[:, 256 + tt * 8:256 + tt * 8 + 8],
                scr[:, 160 + tt:160 + tt + 1], 2.0, ALU.mult, ALU.mult))
        P.drain("vector")
        P.op("vector", lambda v: v.nop(), inc=("dv", 1))
        P.wait("tensor", "dv")
        wt_done = {}
        for tt in range(NKT):
            wt = ps[0:8, 2560 + (tt % 2) * 512: 2560 + (tt % 2) * 512 + 128]
            if tt >= 2:
                P.wait("tensor", "dv", wt_done[tt - 2])
            P.op("tensor", lambda t, tt=tt, wt=wt:
                 t.transpose(wt, scr[:, 256 + tt * 8:256 + tt * 8 + 8], IDF),
                 inc=("pe", 1))
            P.wait("vector", "pe")
            wt_done[tt] = P.op("vector", lambda v, tt=tt, wt=wt:
                 v.tensor_copy(wrow_sb[0:8, tt * 128:tt * 128 + 128], wt),
                 inc=("dv", 1))
        wrow_done = P.val("dv")

        ug_keys = [(j, gT, hb) for j in experts for gT in range(2)
                   for hb in range(NHB)]
        d_keys = [(j, gT, mc) for j in experts for gT in range(2)
                  for mc in range(8)]
        ug_free = {}
        d_free = {}
        ug_emitted = [0]
        d_emitted = [0]

        def pump_ug(upto):
            while ug_emitted[0] < min(upto, len(ug_keys)):
                i = ug_emitted[0]
                j, gT, hb = ug_keys[i]
                sem = "f0" if i % 2 == 0 else "f1"
                if i >= 2 and (i - 2) in ug_free:
                    P.wait("gpsimd", "pe", ug_free[i - 2])
                P.op("gpsimd", lambda g, j=j, hb=hb, i=i:
                     g.dma_start(out=wffa[i % 2], in_=fwug_h.ap()[j, hb]),
                     inc=(sem, 16))
                ug_ld[i] = (sem, P.val(sem))
                ug_emitted[0] += 1

        def pump_d(upto):
            while d_emitted[0] < min(upto, len(d_keys)):
                i = d_emitted[0]
                j, gT, mc = d_keys[i]
                sem = "g0" if i % 2 == 0 else "g1"
                if i >= 2 and (i - 2) in d_free:
                    P.wait("gpsimd", "pe", d_free[i - 2])
                P.op("gpsimd", lambda g, j=j, mc=mc, i=i:
                     g.dma_start(out=wffd[i % 2], in_=fwd_h.ap()[j, mc]),
                     inc=(sem, 16))
                d_ld[i] = (sem, P.val(sem))
                d_emitted[0] += 1

        ug_ld, d_ld = {}, {}
        pump_ug(2)
        pump_d(1)

        wb_done = 0
        for j in experts:
            if j < 7:
                P.wait("tensor", "dv", wrow_done)
                for T in range(NT):
                    P.wait("tensor", "dv")
                    P.wait("tensor", "ac")
                    P.op("tensor", lambda t, j=j, T=T: t.matmul(
                        ps[0:1, 2048:2560], cb_sb[0:8, j:j + 1],
                        wrow_sb[0:8, T * 512:T * 512 + 512],
                        start=True, stop=True), inc=("pe", 1))
                    P.wait("scalar", "pe")
                    P.op("scalar", lambda sc: sc.activation(
                        trow[0:1, 0:512], ps[0:1, 2048:2560], AF.Copy),
                        inc=("ac", 1))
                    P.wait("tensor", "ac")
                    P.op("tensor", lambda t: t.matmul(
                        ps[:, 2560:3072], ONEST, trow[0:1, 0:512],
                        start=True, stop=True), inc=("pe", 1))
                    P.wait("vector", "pe")
                    P.op("vector", lambda v, T=T: v.tensor_copy(
                        wb_sb[:, T * 512:T * 512 + 512], ps[:, 2560:3072]),
                        inc=("dv", 1))
                wb_done = P.val("dv")

            for gT in range(2):
                g0 = gT * 1024
                act_done = {}
                down_prev_dv = P.val("dv")
                silu_done = {}
                for hb in range(NHB):
                    iu = (experts.index(j) * 2 + gT) * NHB + hb
                    pump_ug(iu + 2)
                    wt = wffa[iu % 2]
                    ub0 = bank(0) if hb % 2 == 0 else bank(4)
                    ub1 = bank(1) if hb % 2 == 0 else bank(5)
                    P.wait("tensor", ug_ld[iu][0], ug_ld[iu][1])
                    P.wait("tensor", "dv", xn_ready)
                    if hb >= 2:
                        P.wait("tensor", "dv", act_done[hb - 2])
                    else:
                        P.wait("tensor", "dv", down_prev_dv)
                    for dc in range(NCD):
                        P.op("tensor", lambda t, dc=dc, g0=g0, ub0=ub0, wt=wt:
                             t.matmul(ub0, wt[:, dc * 128:dc * 128 + 128],
                                      xnv(dc, g0, 512),
                                      start=(dc == 0), stop=(dc == NCD - 1)))
                        P.op("tensor", lambda t, dc=dc, g0=g0, ub1=ub1, wt=wt:
                             t.matmul(ub1, wt[:, dc * 128:dc * 128 + 128],
                                      xnv(dc, g0 + 512, 512),
                                      start=(dc == 0), stop=(dc == NCD - 1)))
                    if hb >= 1:
                        P.wait("tensor", "ac", silu_done[hb - 1])
                    for dc in range(NCD):
                        P.op("tensor", lambda t, dc=dc, g0=g0, wt=wt:
                             t.matmul(bank(2),
                                      wt[:, 1024 + dc * 128:1024 + dc * 128 + 128],
                                      xnv(dc, g0, 512),
                                      start=(dc == 0), stop=(dc == NCD - 1)))
                        P.op("tensor", lambda t, dc=dc, g0=g0, wt=wt:
                             t.matmul(bank(3),
                                      wt[:, 1024 + dc * 128:1024 + dc * 128 + 128],
                                      xnv(dc, g0 + 512, 512),
                                      start=(dc == 0), stop=(dc == NCD - 1)),
                             inc=("pe", 1) if dc == NCD - 1 else None)
                    ug_free[iu] = P.val("pe")
                    P.wait("scalar", "pe")
                    P.wait("scalar", "dv")
                    sg0 = scr[:, 1024:1536]
                    sg1 = scr[:, 1536:2048]
                    P.op("scalar", lambda sc, sg0=sg0: sc.activation(
                        sg0, bank(2), AF.Silu))
                    silu_done[hb] = P.op("scalar", lambda sc, sg1=sg1:
                         sc.activation(sg1, bank(3), AF.Silu), inc=("ac", 1))
                    P.wait("vector", "ac")
                    P.op("vector", lambda v, hb=hb, ub0=ub0, sg0=sg0:
                         v.tensor_mul(act_sb[:, hb * 1024:hb * 1024 + 512],
                                      ub0, sg0))
                    act_done[hb] = P.op("vector", lambda v, hb=hb, ub1=ub1, sg1=sg1:
                         v.tensor_mul(act_sb[:, hb * 1024 + 512:hb * 1024 + 1024],
                                      ub1, sg1), inc=("dv", 1))
                a_done = P.val("dv")

                devac = {}
                for mc in (range(8) if not FFN_SKIP_B[0] else []):
                    idd = (experts.index(j) * 2 + gT) * 8 + mc
                    pump_d(idd + 2)
                    wdt = wffd[idd % 2]
                    db0 = bank((mc % 2) * 2)
                    db1 = bank((mc % 2) * 2 + 1)
                    P.wait("tensor", d_ld[idd][0], d_ld[idd][1])
                    P.wait("tensor", "dv", a_done)
                    if mc >= 2:
                        P.wait("tensor", "dv", devac[mc - 2])
                    else:
                        P.wait("tensor", "dv")
                    for hb in range(NHB):
                        P.op("tensor", lambda t, hb=hb, db0=db0, wdt=wdt:
                             t.matmul(db0, wdt[:, hb * 128:hb * 128 + 128],
                                      act_sb[:, hb * 1024:hb * 1024 + 512],
                                      start=(hb == 0), stop=(hb == NHB - 1)))
                        P.op("tensor", lambda t, hb=hb, db1=db1, wdt=wdt:
                             t.matmul(db1, wdt[:, hb * 128:hb * 128 + 128],
                                      act_sb[:, hb * 1024 + 512:hb * 1024 + 1024],
                                      start=(hb == 0), stop=(hb == NHB - 1)),
                             inc=("pe", 1) if hb == NHB - 1 else None)
                    d_free[idd] = P.val("pe")
                    P.wait("vector", "pe")
                    if j < 7:
                        P.wait("vector", "dv", wb_done)
                        P.op("vector", lambda v, db0=db0, g0=g0:
                             v.tensor_mul(scr[:, 0:512], db0,
                                          wb_sb[:, g0:g0 + 512]))
                        P.op("vector", lambda v, db1=db1, g0=g0:
                             v.tensor_mul(scr[:, 512:1024], db1,
                                          wb_sb[:, g0 + 512:g0 + 1024]))
                        P.drain("vector")
                        P.op("vector", lambda v, mc=mc, g0=g0:
                             v.tensor_add(xv(mc, g0, 512), xv(mc, g0, 512),
                                          scr[:, 0:512]))
                        devac[mc] = P.op("vector", lambda v, mc=mc, g0=g0:
                             v.tensor_add(xv(mc, g0 + 512, 512),
                                          xv(mc, g0 + 512, 512),
                                          scr[:, 512:1024]), inc=("dv", 1))
                        P.drain("vector")
                    else:
                        P.op("vector", lambda v, mc=mc, g0=g0, db0=db0:
                             v.tensor_add(xv(mc, g0, 512), xv(mc, g0, 512), db0))
                        devac[mc] = P.op("vector", lambda v, mc=mc, g0=g0, db1=db1:
                             v.tensor_add(xv(mc, g0 + 512, 512),
                                          xv(mc, g0 + 512, 512), db1),
                             inc=("dv", 1))
                        P.drain("vector")
        return P.val("dv")

    def emit_titans(x_xd, x_dv):
        xn_ready = emit_ln(x_xd, x_dv)
        park_done = emit_park(xn_ready)
        P.wait("gpsimd", "xi", park_done)

        def kboff(st, h):
            return (st // 2) * 1024 + h * 64

        def tit_proj(widx, kind, dst):
            P.wait("gpsimd", "pe")
            P.op("gpsimd", lambda g, widx=widx:
                 g.dma_start(out=wres[:], in_=bw_h.ap()[widx]), inc=("wa", 16))
            wld = P.val("wa")
            pidx = widx if widx < 3 else 0
            for qc in range(NCD):
                P.wait("tensor", "wa", wld)
                P.wait("tensor", "dv", xn_ready)
                P.wait("tensor", "dv")
                P.wait("tensor", "ac")
                for T in range(NT):
                    for dc in range(NCD):
                        P.op("tensor", lambda t, qc=qc, T=T, dc=dc:
                             t.matmul(ps[:, T * 512:T * 512 + 512],
                                      wres[:, dc * 1024 + qc * 128:
                                            dc * 1024 + qc * 128 + 128],
                                      xnv(dc, T * 512, 512),
                                      start=(dc == 0), stop=(dc == NCD - 1)),
                             inc=(("pe", 1) if (T == NT - 1 and dc == NCD - 1)
                                  else None))
                if kind == "g":
                    P.wait("scalar", "pe")
                    for T in range(NT):
                        P.op("scalar", lambda sc, qc=qc, T=T, dst=dst:
                             sc.activation(fmv(dst, qc, T * 512, 512),
                                           ps[:, T * 512:T * 512 + 512],
                                           AF.Copy), inc=("ac", 1))
                    continue
                P.wait("vector", "pe")
                w3 = w3p_sb[:, pidx * 8 + qc:pidx * 8 + qc + 1]
                P.op("vector", lambda v, w3=w3: v.tensor_scalar(
                    scr[:, 0:2048], ps[:, 0:2048], w3, None, ALU.mult))
                P.drain("vector")
                for k in (1, 2, 3):
                    wk = conv_sb[:, pidx * 32 + qc * 4 + (3 - k):
                                 pidx * 32 + qc * 4 + (3 - k) + 1]
                    P.op("vector", lambda v, k=k, wk=wk: v.scalar_tensor_tensor(
                        scr[:, k:2048], ps[:, 0:2048 - k], wk, scr[:, k:2048],
                        ALU.mult, ALU.add))
                    P.drain("vector")
                P.op("vector", lambda v: v.nop(), inc=("dv", 1))
                P.wait("scalar", "dv")
                P.op("scalar", lambda sc: sc.activation(
                    ps[:, 0:2048], scr[:, 0:2048], AF.Silu), inc=("ac", 1))
                P.drain("scalar")
                if kind == "v":
                    P.wait("vector", "ac")
                    P.op("vector", lambda v: v.tensor_copy(scr[:, 0:2048],
                                                           ps[:, 0:2048]),
                         inc=("dv", 1))
                    P.drain("vector")
                    P.wait("tensor", "dv")
                    tr_done = {}
                    for b in range(NKT):
                        trp = ps[:, 2048 + (b % 2) * 512:
                                 2048 + (b % 2) * 512 + 128]
                        if b >= 2:
                            P.wait("tensor", "dv", tr_done[b - 2])
                        P.op("tensor", lambda t, b=b, trp=trp:
                             t.transpose(trp, scr[:, b * 128:b * 128 + 128], IDF),
                             inc=("pe", 1))
                        P.wait("vector", "pe")
                        kb = (qc * 2048 + b * 128) // 128
                        blk, = (kb % 16,)
                        tr_done[b] = P.op(
                            "vector", lambda v, qc=qc, blk=blk, trp=trp:
                            v.tensor_copy(
                                vtm_t[:, blk * 1024 + 2 * qc * 64:
                                      blk * 1024 + 2 * qc * 64 + 128],
                                trp), inc=("dv", 1))
                    continue
                for T in range(NT):
                    P.wait("scalar", "ac")
                    P.wait("scalar", "dv")
                    P.op("scalar", lambda sc, T=T: sc.activation(
                        scr[:, 0:512], ps[:, T * 512:T * 512 + 512], AF.Square),
                        inc=("ac", 1))
                    P.wait("tensor", "ac")
                    P.op("tensor", lambda t: t.matmul(
                        ps[0:2, 2048:2560], MASK2, scr[:, 0:512],
                        start=True, stop=True), inc=("pe", 1))
                    P.wait("scalar", "pe")
                    P.op("scalar", lambda sc: sc.activation(
                        trow[0:2, 0:512], ps[0:2, 2048:2560], AF.Ln,
        bias=cf_sb[0:2, CC["eps_l2"]:CC["eps_l2"] + 1]))
                    P.drain("scalar")
                    P.op("scalar", lambda sc: sc.activation(
                        trow[0:2, 0:512], trow[0:2, 0:512], AF.Exp, scale=-0.5),
                        inc=("ac", 1))
                    P.drain("scalar")
                    P.wait("tensor", "ac")
                    P.op("tensor", lambda t: t.matmul(
                        ps[:, 2560:3072], SEL2T, trow[0:2, 0:512],
                        start=True, stop=True), inc=("pe", 1))
                    P.wait("scalar", "pe")
                    P.op("scalar", lambda sc: sc.activation(
                        scr[:, 512:1024], ps[:, 2560:3072], AF.Copy),
                        inc=("ac", 1))
                    P.wait("vector", "ac")
                    P.op("vector", lambda v, qc=qc, T=T, dst=dst:
                         v.tensor_mul(fmv(dst, qc, T * 512, 512),
                                      ps[:, T * 512:T * 512 + 512],
                                      scr[:, 512:1024]), inc=("dv", 1))
                    P.drain("vector")

        tit_proj(0, "qk", bufA)
        tit_proj(1, "qk", bufB)
        tit_proj(2, "v", None)
        tit_proj(3, "g", gate_sb)
        projs_dv = P.val("dv")
        projs_ac = P.val("ac")

        P.wait("tensor", "dv", projs_dv)
        P.wait("tensor", "ac", projs_ac)
        ktr_done = {}
        for qc in range(NCD):
            for b in range(NKT):
                i = qc * NKT + b
                trp = psb[:, (i % 2) * 512:(i % 2) * 512 + 128]
                if i >= 1:
                    P.wait("tensor", "dv", ktr_done[i - 1])
                P.op("tensor", lambda t, qc=qc, b=b, trp=trp:
                     t.transpose(trp, fmv(bufB, qc, b * 128, 128), IDB),
                     inc=("pe", 1))
                P.wait("vector", "pe")
                ktr_done[i] = P.op("vector", lambda v, qc=qc, b=b, trp=trp:
                     v.tensor_copy(
                         ktm_sb[:, b * 1024 + 2 * qc * 64:
                                b * 1024 + 2 * qc * 64 + 128], trp),
                     inc=("dv", 1))
        ktm_done = P.val("dv")

        if TIT_ONLY[0] == "gen":
            unpark_done = emit_unpark(ktm_done)
            P.wait("vector", "xi", unpark_done)
            P.op("vector", lambda v: v.nop(), inc=("dv", 1))
            return P.val("dv")

        P.wait("sync", "dv", projs_dv)
        P.wait("sync", "ac", projs_ac)
        P.op("sync", lambda s: s.dma_start(out=w0_sb[:], in_=w0_h.ap()),
             inc=("xi", 16))
        w0_ld = P.val("xi")
        P.wait("tensor", "xi", w0_ld)
        P.wait("tensor", "dv", ktm_done)
        P.op("tensor", lambda t: t.matmul(
            ps[:, 0:512], IDF, w0_sb[:], start=True, stop=True), inc=("pe", 1))
        P.wait("scalar", "pe")
        P.op("scalar", lambda sc: sc.activation(wsb[:], ps[:, 0:512], AF.Copy),
             inc=("ac", 1))
        wsb_ready = P.val("ac")

        for st in range(TIT_STEPS[0] if TIT_STEPS[0] is not None else NSUP):
            tb = 64 * (st % 2)
            t0 = st * 64
            P.wait("tensor", "dv")
            head_order = [hh for hh in range(H) if hh % 2 == 0] +                          [hh for hh in range(H) if hh % 2 == 1]
            for hi, h in enumerate(head_order):
                ch, par = h // 2, h % 2
                qb = 512 + par * 512 + (h // 2) * 64
                P.op("tensor", lambda t, ch=ch, par=par, qb=qb, tb=tb, t0=t0, h=h,
                     hi=hi:
                     t.matmul(ps[tb:tb + 64, qb:qb + 64],
                              fmv(bufB, ch, t0, 64)[par * 64:par * 64 + 64, :],
                              fmv(bufA, ch, t0, 64)[par * 64:par * 64 + 64, :],
                              start=(hi % 8 == 0), stop=(hi % 8 == 7),
                              skip_group_check=True),
                     inc=("pe", 1) if hi == H - 1 else None)
            P.wait("vector", "pe")
            if TIT_CUT[0] == "mmonly":
                P.op("vector", lambda v: v.nop(), inc=("dv", 1))
            for hh in (range(8) if TIT_CUT[0] != "mmonly" else []):
                P.op("vector", lambda v, tb=tb, hh=hh: v.tensor_mul(
                    qkt_sb[tb:tb + 64, hh * 64:hh * 64 + 64],
                    ps[tb:tb + 64, 512 + hh * 64:512 + hh * 64 + 64],
                    BMASK8[tb:tb + 64, :]))
            for hh in (range(8) if TIT_CUT[0] != "mmonly" else []):
                P.op("vector", lambda v, tb=tb, hh=hh: v.tensor_mul(
                    qkt_sb[tb:tb + 64, 512 + hh * 64:512 + hh * 64 + 64],
                    ps[tb:tb + 64, 1024 + hh * 64:1024 + hh * 64 + 64],
                    BMASK8[tb:tb + 64, :]),
                    inc=(("dv", 1) if hh == 7 else None))
            P.drain("vector")
            qkt_dv = P.val("dv")
            if TIT_CUT[0] == "qkt":
                continue
            if TIT_CUT[0] == "mmonly":
                continue

            P.wait("tensor", "dv", qkt_dv)
            P.wait("tensor", "ac", wsb_ready)

            def o_slice(h):
                par = h % 2
                base = 1536 if par == 0 else 3072
                return ps[par * 64:par * 64 + 64,
                          base + (h // 2) * 64:base + (h // 2) * 64 + 64]

            for h in range(H):
                ch, par = h // 2, h % 2
                P.op("tensor", lambda t, h=h, ch=ch, par=par, t0=t0:
                     t.matmul(o_slice(h),
                              wsb[par * 64:par * 64 + 64,
                                  (h // 2) * 64:(h // 2) * 64 + 64],
                              fmv(bufA, ch, t0, 64)[par * 64:par * 64 + 64, :],
                              start=(h <= 1), stop=False, skip_group_check=True),
                     inc=("pe", 1) if h == H - 1 else None)
            P.wait("tensor", "pe")
            for h in range(H):
                par = h % 2
                qcol = par * 512 + (h // 2) * 64
                P.op("tensor", lambda t, h=h, par=par, tb=tb, st=st, qcol=qcol:
                     t.matmul(o_slice(h),
                              vtm_t[tb:tb + 64, kboff(st, h):kboff(st, h) + 64],
                              qkt_sb[tb:tb + 64, qcol:qcol + 64],
                              start=False, stop=(h >= H - 2),
                              skip_group_check=True),
                     inc=("pe", 1) if h == H - 1 else None)
            o_pe = P.val("pe")
            for h in range(H):
                par = h % 2
                P.op("tensor", lambda t, h=h, par=par, tb=tb, st=st:
                     t.matmul(ps[par * 64:par * 64 + 64,
                                 (h // 2) * 64:(h // 2) * 64 + 64],
                              ktm_sb[tb:tb + 64, kboff(st, h):kboff(st, h) + 64],
                              vtm_t[tb:tb + 64, kboff(st, h):kboff(st, h) + 64],
                              start=False, stop=(h == H - 1),
                              skip_group_check=True),
                     inc=("pe", 1) if h == H - 1 else None)
            wupd_pe = P.val("pe")
            P.wait("scalar", "pe", wupd_pe)
            P.op("scalar", lambda sc: sc.activation(wsb[:], ps[:, 0:512], AF.Copy),
                 inc=("ac", 1))
            wsb_ready = P.val("ac")
            if TIT_CUT[0] == "w":
                P.wait("vector", "pe", o_pe)
                P.op("vector", lambda v: v.tensor_copy(scr[:, 0:512],
                                                       ps[:, 1536:2048]),
                     inc=("dv", 1))
                P.drain("vector")
                continue

            P.wait("vector", "pe", o_pe)
            P.op("vector", lambda v: v.tensor_copy(scr[0:64, 0:512],
                                                   ps[0:64, 1536:2048]))
            P.op("vector", lambda v: v.tensor_copy(scr[64:128, 0:512],
                                                   ps[64:128, 3072:3584]),
                 inc=("dv", 1))
            P.drain("vector")
            P.wait("tensor", "dv")
            P.wait("tensor", "ac")
            P.op("tensor", lambda t: t.matmul(
                ps[0:2, 2048:2560], MASK2, scr[:, 0:512], start=True, stop=True),
                inc=("pe", 1))
            P.wait("scalar", "pe")
            P.op("scalar", lambda sc: sc.activation(
                trow[0:2, 0:512], ps[0:2, 2048:2560], AF.Copy, scale=1.0 / 64),
                inc=("ac", 1))
            P.wait("tensor", "ac")
            P.wait("tensor", "dv")
            P.op("tensor", lambda t: t.matmul(
                ps[:, 2560:3072], SEL2T, trow[0:2, 0:512], start=True, stop=True),
                inc=("pe", 1))
            P.wait("vector", "pe")
            P.op("vector", lambda v: v.tensor_sub(scr[:, 512:1024], scr[:, 0:512],
                                                  ps[:, 2560:3072]),
                 inc=("dv", 1))
            P.drain("vector")
            P.wait("scalar", "dv")
            P.op("scalar", lambda sc: sc.activation(
                scr[:, 1024:1536], scr[:, 512:1024], AF.Square), inc=("ac", 1))
            P.wait("tensor", "ac")
            P.op("tensor", lambda t: t.matmul(
                ps[0:2, 2048:2560], MASK2, scr[:, 1024:1536],
                start=True, stop=True), inc=("pe", 1))
            P.wait("scalar", "pe")
            P.op("scalar", lambda sc: sc.activation(
                scr[0:2, 0:512], ps[0:2, 2048:2560], AF.Ln,
        bias=cf_sb[0:2, CC["eps_gn"]:CC["eps_gn"] + 1]))
            P.drain("scalar")
            P.op("scalar", lambda sc: sc.activation(
                trow[0:2, 0:512], scr[0:2, 0:512], AF.Exp, scale=-0.5,
                bias=cf_sb[0:2, CC["ln8"]:CC["ln8"] + 1]), inc=("ac", 1))
            P.drain("scalar")
            P.wait("tensor", "ac")
            P.op("tensor", lambda t: t.matmul(
                ps[:, 2560:3072], SEL2T, trow[0:2, 0:512],
                start=True, stop=True), inc=("pe", 1))
            P.wait("vector", "pe")
            P.op("vector", lambda v: v.tensor_mul(
                scr[:, 512:1024], scr[:, 512:1024], ps[:, 2560:3072]))
            P.drain("vector")
            P.op("vector", lambda v: v.tensor_scalar(
                scr[:, 512:1024], scr[:, 512:1024], gb_sb[:, 0:1], gb_sb[:, 1:2],
                ALU.mult, ALU.add))
            P.drain("vector")
            P.op("vector", lambda v, t0=t0: v.tensor_mul(
                ofm_sb[:].rearrange("p (c t) -> p c t", t=L)[:, :, t0:t0 + 64],
                scr[:, 512:1024].rearrange("p (c t) -> p c t", t=64),
                gate_sb[:].rearrange("p (c t) -> p c t", t=L)[:, :, t0:t0 + 64]),
                inc=("dv", 1))
            P.drain("vector")
        og_done = P.val("dv")

        if TIT_ONLY[0] == "scan":
            unpark_done = emit_unpark(og_done)
            P.wait("vector", "xi", unpark_done)
            P.op("vector", lambda v: v.nop(), inc=("dv", 1))
            return P.val("dv")

        P.wait("gpsimd", "pe")
        P.op("gpsimd", lambda g: g.dma_start(out=wresA[:], in_=bw_h.ap()[4]),
             inc=("wx", 16))
        wo_ld = P.val("wx")
        unpark_done = emit_unpark(og_done)

        P.wait("tensor", "xi", unpark_done)
        P.wait("tensor", "wx", wo_ld)
        P.wait("tensor", "dv", og_done)
        prev = {}
        n = 0
        for dc in range(NCD):
            for T in range(NT):
                pb = bank(n % 2)
                if n >= 2:
                    P.wait("tensor", "dv", prev[n - 2])
                for fc in range(NCD):
                    P.op("tensor", lambda t, dc=dc, T=T, fc=fc, pb=pb:
                         t.matmul(pb,
                                  wresA[:, fc * 1024 + dc * 128:
                                        fc * 1024 + dc * 128 + 128],
                                  fmv(ofm_sb, fc, T * 512, 512),
                                  start=(fc == 0), stop=(fc == NCD - 1)),
                         inc=("pe", 1) if fc == NCD - 1 else None)
                P.wait("vector", "pe")
                prev[n] = P.op("vector", lambda v, dc=dc, T=T, pb=pb:
                     v.tensor_add(xv(dc, T * 512, 512), xv(dc, T * 512, 512), pb),
                     inc=("dv", 1))
                n += 1
        P.drain("vector")
        return P.val("dv")

    for s in range(2):
        if s == 1:
            P.wait("sync", "out", 16)
            P.wait("sync", "dv")
            if DEBUG_BARRIER[0]:
                for e in ("tensor", "scalar", "vector", "gpsimd"):
                    P.wait(e, "out", 16)
                    P.wait(e, "dv")
                    P.wait(e, "ac")
                    P.wait(e, "pe")
        P.wait("sync", "xi")
        P.op("sync", lambda sy, s=s: sy.dma_start(out=x_sb[:], in_=x_h.ap()[s]),
             inc=("xi", 16))
        x_xd = P.val("xi")
        x_dv = P.val("dv")
        for ph in PHASES:
            if ph == "attn_a":
                x_dv = emit_attn_a(x_xd, x_dv)
                x_xd = P.val("xi")
            elif ph in ("ffn0", "ffn1"):
                x_dv = emit_ffn(x_xd, x_dv)
                x_xd = P.val("xi")
            elif ph == "ffns":
                x_dv = emit_ffn(x_xd, x_dv, experts=(7,))
                x_xd = P.val("xi")
            elif ph == "ffn1e":
                x_dv = emit_ffn(x_xd, x_dv, experts=(0, 7))
                x_xd = P.val("xi")
            elif ph == "ffnr":
                x_dv = emit_ffn(x_xd, x_dv, experts=())
                x_xd = P.val("xi")
            elif ph == "titans":
                x_dv = emit_titans(x_xd, x_dv)
                x_xd = P.val("xi")
            elif ph == "lncheck":
                xn_ready = emit_ln(x_xd, x_dv)
                P.wait("vector", "dv", xn_ready)
                for dc in range(NCD):
                    P.op("vector", lambda v, dc=dc: v.tensor_copy(
                        xv(dc, 0, L), xnv(dc, 0, L)), inc=("dv", 1))
                P.drain("vector")
                x_dv = P.val("dv")
        P.wait("sync", "dv", x_dv)
        P.op("sync", lambda sy, s=s: sy.dma_start(out=y_h.ap()[s], in_=x_sb[:]),
             inc=("out", 16))
    P.wait("sync", "out", P.val("out"))
    if DEBUG_DUMP[0]:
        P.wait("gpsimd", "out", P.val("out"))
        P.wait("gpsimd", "dv")
        P.wait("gpsimd", "ac")
        P.wait("gpsimd", "pe")
        which = DEBUG_DUMP[0]
        if which == "xn_act":
            P.op("gpsimd", lambda g: g.dma_start(
                out=y_h.ap()[0].bitcast(BF16)[:, 0:16384], in_=xn_sb[:]),
                 inc=("out", 16))
            P.op("gpsimd", lambda g: g.dma_start(
                out=y_h.ap()[1].bitcast(BF16)[:, 0:16384],
                in_=act_sb[:, 0:16384]), inc=("out", 16))
        elif which == "rows":
            P.op("gpsimd", lambda g: g.dma_start(out=y_h.ap()[0][:, 0:1024],
                                                 in_=trow[:]), inc=("out", 16))
            P.op("gpsimd", lambda g: g.dma_start(out=y_h.ap()[0][:, 1024:3072],
                                                 in_=scr[:]), inc=("out", 16))
        elif which == "wff":
            P.op("gpsimd", lambda g: g.dma_start(out=y_h.ap()[0][:, 0:4096],
                                                 in_=wffa_t[:]), inc=("out", 16))
            P.op("gpsimd", lambda g: g.dma_start(out=y_h.ap()[0][:, 4096:12288],
                                                 in_=wffd_t[:]), inc=("out", 16))
        P.wait("sync", "out", P.val("out"))

    import contextlib
    with contextlib.ExitStack() as ctx:
        sems = {n: ctx.enter_context(nc.semaphore("s_" + n)) for n in sem_names}
        with nc.Block() as block:
            block.sync(lambda e: P.replay("sync", e, sems))
            block.tensor(lambda e: P.replay("tensor", e, sems))
            block.vector(lambda e: P.replay("vector", e, sems))
            block.scalar(lambda e: P.replay("scalar", e, sems))
            block.gpsimd(lambda e: P.replay("gpsimd", e, sems))
    return nc


def kernel(**inputs):
    g = stage_inputs(inputs)
    nc = build_program()
    shared = {k: g[k] for k in ("aw", "bw", "fwug", "fwd", "wr", "bo", "convw",
                                "w0", "gb", "constf", "constb")}
    in_maps = []
    for i in range(8):
        m = dict(shared)
        m["x"] = g["x_cores"][i]
        in_maps.append(m)
    res = run_bass_kernel_spmd(nc, in_maps, core_ids=list(range(8)))
    if DEBUG_DUMP[0]:
        return [r["y"] for r in res.results]
    out = np.empty((16, L, D), dtype=np.float32)
    for i in range(8):
        y = res.results[i]["y"]
        y = y.reshape(2, 128, 8, L).transpose(0, 2, 1, 3).reshape(2, D, L)
        out[2 * i:2 * i + 2] = y.transpose(0, 2, 1)
    return out


# version 03 (final)
# speedup vs baseline: 1.0843x; 1.0843x over previous
import math
import numpy as np
import ml_dtypes
from collections import defaultdict
import concourse.bass as bass
import concourse.mybir as mybir
from concourse.bass_utils import run_bass_kernel_spmd

F32 = mybir.dt.float32
BF16 = mybir.dt.bfloat16
AF = mybir.ActivationFunctionType
ALU = mybir.AluOpType
AX = mybir.AxisListType
BFNP = ml_dtypes.bfloat16

L = 2048
D = 1024
NCD = 8
H = 16
NHB = 32
NT = 4
NKT = 16
NSUP = 32

PHASES = ["attn_a", "ffn0", "titans", "ffn1"]
DEBUG_DUMP = [None]
DEBUG_BARRIER = [False]
FFN_SKIP_B = [False]
TIT_ONLY = [None]
TIT_STEPS = [None]
TIT_CUT = [None]
TIT_HEADS = [None]


class Plan:
    def __init__(self):
        self.items = []
        self.cnt = defaultdict(int)

    def op(self, eng, fn, inc=None):
        if inc is not None:
            self.cnt[inc[0]] += inc[1]
        self.items.append((eng, "op", (fn, inc)))
        return self.cnt[inc[0]] if inc else None

    def wait(self, eng, sem, val=None):
        v = self.cnt[sem] if val is None else val
        if v > 0:
            self.items.append((eng, "wait", (sem, v)))
        return v

    def drain(self, eng):
        self.items.append((eng, "op", (lambda e: e.drain(), None)))

    def val(self, sem):
        return self.cnt[sem]

    def replay(self, eng_name, eng_obj, sems):
        for eng, kind, payload in self.items:
            if eng != eng_name:
                continue
            if kind == "op":
                fn, inc = payload
                ins = fn(eng_obj)
                if inc is not None:
                    ins.then_inc(sems[inc[0]], inc[1])
            else:
                sem, v = payload
                eng_obj.wait_ge(sems[sem], v)


def _stage_sq(w):
    return w.reshape(8, 128, 1024).transpose(1, 0, 2).reshape(128, 8192)


def _build_consts():
    cols = {}
    parts = []
    c = [0]

    def add(name, arr):
        cols[name] = c[0]
        parts.append(arr.astype(np.float32))
        c[0] += arr.shape[1]

    add("ident", np.eye(128))
    add("ones1", np.ones((128, 1)))
    o = np.zeros((128, 128)); o[0, :] = 1; o[64, :] = 1
    add("onesT", o)
    m2 = np.zeros((128, 2)); m2[0:64, 0] = 1; m2[64:128, 1] = 1
    add("mask2", m2)
    s2 = np.zeros((128, 128)); s2[0, 0:64] = 1; s2[1, 64:128] = 1
    add("sel2T", s2)
    m16 = np.zeros((128, 8 * 16))
    for ch in range(8):
        m16[0:64, ch * 16 + 2 * ch] = 1
        m16[64:128, ch * 16 + 2 * ch + 1] = 1
    add("m16", m16)
    bm = np.zeros((64, 64))
    for j in range(64):
        for i in range(64):
            if (j // 16 < i // 16) or (j // 16 == i // 16 and j <= i):
                bm[j, i] = 1
    bm2 = np.concatenate([bm, bm], axis=0)
    add("bmask8", bm2)
    import math as _m
    add("eps_ln", np.full((128, 1), 1e-5))
    add("eps_l2", np.full((128, 1), 1e-12))
    add("eps_q", np.full((128, 1), 1e-30))
    add("eps_gn", np.full((128, 1), 64e-5))
    add("ln8", np.full((128, 1), _m.log(8.0)))
    return np.ascontiguousarray(np.concatenate(parts, axis=1)), cols


CF_ARR, CC = _build_consts()


def stage_inputs(inp):
    g = {}
    X = inp["L0_attn_a_x"]
    xt = X.transpose(0, 2, 1)
    g["x_cores"] = [
        np.ascontiguousarray(
            xt[2 * i:2 * i + 2].reshape(2, 8, 128, L).transpose(0, 2, 1, 3)
            .reshape(2, 128, 8 * L))
        for i in range(8)
    ]
    g["aw"] = np.ascontiguousarray(np.stack([
        _stage_sq(inp["L0_attn_a_Wq"]), _stage_sq(inp["L0_attn_a_Wk"]),
        _stage_sq(inp["L0_attn_a_Wv"]), _stage_sq(inp["L0_attn_a_Wo"])]))
    g["bw"] = np.ascontiguousarray(np.stack([
        _stage_sq(np.ascontiguousarray(inp["L0_attn_b_Wq"].T)),
        _stage_sq(np.ascontiguousarray(inp["L0_attn_b_Wk"].T)),
        _stage_sq(np.ascontiguousarray(inp["L0_attn_b_Wv"].T)),
        _stage_sq(np.ascontiguousarray(inp["L0_attn_b_Wg"].T)),
        _stage_sq(np.ascontiguousarray(inp["L0_attn_b_Wo"].T))]))
    WU = np.concatenate([inp["L0_ffn_Wu"], inp["L0_ffn_Wu_s"][None]], 0)
    WG = np.concatenate([inp["L0_ffn_Wg"], inp["L0_ffn_Wg_s"][None]], 0)
    WD = np.concatenate([inp["L0_ffn_Wd"], inp["L0_ffn_Wd_s"][None]], 0)
    u5 = WU.reshape(8, 8, 128, 32, 128).transpose(0, 3, 2, 1, 4)
    g5 = WG.reshape(8, 8, 128, 32, 128).transpose(0, 3, 2, 1, 4)
    g["fwug"] = np.ascontiguousarray(
        np.concatenate([u5.reshape(8, 32, 128, 1024),
                        g5.reshape(8, 32, 128, 1024)], axis=3))
    g["fwd"] = np.ascontiguousarray(
        WD.reshape(8, 32, 128, 8, 128).transpose(0, 3, 2, 1, 4)
        .reshape(8, 8, 128, 4096))
    g["wr"] = np.ascontiguousarray(
        inp["L0_ffn_Wr"].reshape(8, 128, 8).transpose(1, 0, 2).reshape(128, 64))
    g["bo"] = np.ascontiguousarray(inp["L0_attn_a_bo"].reshape(8, 128).T)
    cw = np.stack([inp["L0_attn_b_conv_q"], inp["L0_attn_b_conv_k"],
                   inp["L0_attn_b_conv_v"]])
    g["convw"] = np.ascontiguousarray(
        cw.reshape(3, 8, 128, 4).transpose(2, 0, 1, 3).reshape(128, 96))
    w0 = np.zeros((128, 512), dtype=np.float32)
    wi = inp["L0_attn_b_W_init"][0]
    for h in range(16):
        w0[(h % 2) * 64:(h % 2) * 64 + 64,
           (h // 2) * 64:(h // 2) * 64 + 64] = wi[h]
    g["w0"] = w0
    gb = np.zeros((128, 2), dtype=np.float32)
    gb[:, 0] = np.tile(inp["L0_attn_b_ln_gamma"], 2)
    gb[:, 1] = np.tile(inp["L0_attn_b_ln_beta"], 2)
    g["gb"] = gb
    g["constf"] = CF_ARR
    cb = np.concatenate([np.eye(128, dtype=BFNP),
                         np.ones((128, 1), dtype=BFNP)], axis=1)
    g["constb"] = np.ascontiguousarray(cb)
    return g


def build_program():
    nc = bass.Bass("TRN2", target_bir_lowering=False,
                   dynamic_dma_scratch_size=16384)
    P = Plan()

    x_h = nc.dram_tensor("x", [2, 128, NCD * L], F32, kind="ExternalInput")
    y_h = nc.dram_tensor("y", [2, 128, NCD * L], F32, kind="ExternalOutput")
    aw_h = nc.dram_tensor("aw", [4, 128, 8192], F32, kind="ExternalInput")
    bw_h = nc.dram_tensor("bw", [5, 128, 8192], F32, kind="ExternalInput")
    fwug_h = nc.dram_tensor("fwug", [8, 32, 128, 2048], F32, kind="ExternalInput")
    fwd_h = nc.dram_tensor("fwd", [8, 8, 128, 4096], F32, kind="ExternalInput")
    wr_h = nc.dram_tensor("wr", [128, 64], F32, kind="ExternalInput")
    bo_h = nc.dram_tensor("bo", [128, 8], F32, kind="ExternalInput")
    conv_h = nc.dram_tensor("convw", [128, 96], F32, kind="ExternalInput")
    w0_h = nc.dram_tensor("w0", [128, 512], F32, kind="ExternalInput")
    gb_h = nc.dram_tensor("gb", [128, 2], F32, kind="ExternalInput")
    cf_h = nc.dram_tensor("constf", [128, CF_ARR.shape[1]], F32,
                          kind="ExternalInput")
    cb_h = nc.dram_tensor("constb", [128, 129], BF16, kind="ExternalInput")
    park_h = nc.dram_tensor("park", [128, NCD * L], F32, kind="Internal")

    BASE = 16384 + 32
    off = [BASE]

    def at(name, shape, dtype, addr=None):
        nbytes = int(np.prod(shape[1:])) * mybir.dt.size(dtype)
        if addr is None:
            addr = off[0]
            off[0] = addr + (nbytes + 31) // 32 * 32
        return nc.alloc_sbuf_tensor_at(name, list(shape), dtype, offset=addr)

    cf_sb = at("cf_sb", [128, CF_ARR.shape[1]], F32)
    cb_sb = at("cb_sb", [128, 129], BF16)
    wr_sb = at("wr_sb", [128, 64], BF16)
    bo_sb = at("bo_sb", [128, 8], F32)
    conv_sb = at("conv_sb", [128, 96], F32)
    w3p_sb = at("w3p_sb", [128, 24], F32)
    gb_sb = at("gb_sb", [128, 2], F32)
    wsb = at("wsb", [128, 512], BF16)
    scr = at("scr", [128, 2048], F32)
    trow = at("trow", [128, 512], F32)
    SCR_ADDR = nc.lookup_mloc(scr).addr
    qkt_sb = at("qkt_sb", [128, 1024], BF16, SCR_ADDR + 1536 * 4)
    w0_sb = at("w0_sb", [128, 512], F32, SCR_ADDR + 1536 * 4)

    X0 = off[0]
    x_sb = at("x_sb", [128, NCD * L], F32, X0)
    XN0 = X0 + NCD * L * 4
    xn_sb = at("xn_sb", [128, NCD * L], BF16, XN0)
    A0 = XN0 + NCD * L * 2
    bufA = at("bufA", [128, NCD * L], BF16, A0)
    B0 = A0 + NCD * L * 2
    bufB = at("bufB", [128, NCD * L], BF16, B0)
    C0 = B0 + NCD * L * 2
    bufC = at("bufC", [128, 16 * 1040], BF16, C0)
    CEND = C0 + 16 * 1040 * 2
    assert CEND - 16384 <= 224 * 1024 - 64, f"sbuf overflow {CEND}"

    wres = at("wres", [128, 8192], BF16, X0)
    wres2 = at("wres2", [128, 8192], BF16, X0 + 16384)
    qn_sb = at("qn_sb", [64, 2048], F32, X0 + 32768)
    kn_sb = at("kn_sb", [64, 2048], F32, X0 + 40960)
    bb_sb = at("bb_sb", [64, 2048], F32, X0 + 49152)
    mk_sb = at("mk_sb", [64, 32], F32, X0 + 57344)
    pb_sb = at("pb_sb", [128, 3 * 512], BF16, X0 + 57600)
    ktm_sb = at("ktm_sb", [128, 16 * 1024], BF16, X0)
    gate_sb = at("gate_sb", [128, NCD * L], BF16, X0 + 32768)
    ofm_sb = at("ofm_sb", [128, NCD * L], BF16, XN0)
    act_sb = at("act_sb", [128, NHB * 1024], BF16, A0)
    wresA = at("wresA", [128, 8192], BF16, A0)
    wresC = at("wresC", [128, 8192], BF16, C0)
    wffa_t = at("wffa", [128, 2 * 2048], BF16, C0)
    wffd_t = at("wffd", [128, 2 * 4096], BF16, C0 + 8192)
    wffa = [wffa_t[:, 0:2048], wffa_t[:, 2048:4096]]
    wffd = [wffd_t[:, 0:4096], wffd_t[:, 4096:8192]]
    wb_sb = at("wb_sb", [128, L], BF16, C0 + 24576)
    wrow_sb = at("wrow_sb", [8, L], BF16, C0 + 28672)
    vtm_t = at("vtm_t", [128, 16 * 1024], BF16, C0)

    ps = nc.alloc_psum_tensor("ps", [128, 3584], F32)
    psb = nc.alloc_psum_tensor("psb", [128, 1024], BF16)

    sem_names = ["ci", "cw", "xi", "wa", "wb", "wx",
                 "f0", "f1", "g0", "g1", "pe", "dv", "ac", "out"]

    def bank(i, w=512):
        return ps[:, i * 512: i * 512 + w]

    IDF = cf_sb[:, CC["ident"]:CC["ident"] + 128]
    ONES1 = cf_sb[:, CC["ones1"]:CC["ones1"] + 1]
    ONEST = cf_sb[0:1, CC["onesT"]:CC["onesT"] + 128]
    ONEST64 = cf_sb[64:65, CC["onesT"]:CC["onesT"] + 128]
    MASK2 = cf_sb[:, CC["mask2"]:CC["mask2"] + 2]
    SEL2T = cf_sb[0:2, CC["sel2T"]:CC["sel2T"] + 128]
    BMASK8 = cf_sb[:, CC["bmask8"]:CC["bmask8"] + 64]
    IDB = cb_sb[:, 0:128]
    ONESB = cb_sb[:, 128:129]

    def m16(ch):
        return cf_sb[:, CC["m16"] + ch * 16: CC["m16"] + ch * 16 + 16]

    def xv(dc, t0, n):
        return x_sb[:, dc * L + t0: dc * L + t0 + n]

    def xnv(dc, t0, n):
        return xn_sb[:, dc * L + t0: dc * L + t0 + n]

    def fmv(buf, ch, t0, n):
        return buf[:, ch * L + t0: ch * L + t0 + n]

    P.op("sync", lambda s: s.dma_start(out=cf_sb[:], in_=cf_h.ap()), inc=("ci", 16))
    P.op("sync", lambda s: s.dma_start(out=bo_sb[:], in_=bo_h.ap()), inc=("ci", 16))
    P.op("sync", lambda s: s.dma_start(out=conv_sb[:], in_=conv_h.ap()), inc=("ci", 16))
    P.op("sync", lambda s: s.dma_start(out=gb_sb[:], in_=gb_h.ap()), inc=("ci", 16))
    P.op("gpsimd", lambda g: g.dma_start(out=cb_sb[:], in_=cb_h.ap()), inc=("cw", 16))
    P.op("gpsimd", lambda g: g.dma_start(out=wr_sb[:], in_=wr_h.ap()), inc=("cw", 16))
    INIT_CI = P.val("ci")
    INIT_CW = P.val("cw")
    P.wait("scalar", "ci", INIT_CI)
    P.wait("vector", "ci", INIT_CI)
    P.wait("tensor", "ci", INIT_CI)
    P.wait("tensor", "cw", INIT_CW)
    P.op("vector", lambda v: v.tensor_scalar(
        w3p_sb[:],
        conv_sb[:].rearrange("p (a k) -> p a k", k=4)[:, :, 3],
        1.0, None, ALU.add), inc=("dv", 1))
    P.drain("vector")

    def emit_ln(x_ready_xd, x_ready_dv):
        P.wait("tensor", "xi", x_ready_xd)
        P.wait("tensor", "dv", x_ready_dv)
        P.wait("scalar", "xi", x_ready_xd)
        P.wait("scalar", "dv", x_ready_dv)
        P.wait("vector", "xi", x_ready_xd)
        for T in range(NT):
            t0 = T * 512
            sum_ps = ps[0:1, 0:512]
            sq_ps = ps[0:1, 512:1024]
            sq_done = {}
            for dc in range(NCD):
                s = scr[:, (dc % 2) * 512:(dc % 2) * 512 + 512]
                if dc >= 2:
                    P.wait("scalar", "pe", sq_done[dc - 2])
                elif T > 0:
                    P.wait("scalar", "pe")
                P.op("scalar", lambda sc, s=s, dc=dc, t0=t0:
                     sc.activation(s, xv(dc, t0, 512), AF.Square), inc=("ac", 1))
                P.wait("tensor", "ac")
                P.op("tensor", lambda t, dc=dc, t0=t0, sum_ps=sum_ps:
                     t.matmul(sum_ps, ONES1, xv(dc, t0, 512),
                              start=(dc == 0), stop=(dc == NCD - 1)))
                sq_done[dc] = P.op("tensor", lambda t, s=s, dc=dc, sq_ps=sq_ps:
                                   t.matmul(sq_ps, ONES1, s,
                                            start=(dc == 0), stop=(dc == NCD - 1)),
                                   inc=("pe", 1))
            m_r = trow[0:1, 0:512]
            v_r = scr[0:1, 1024:1536]
            mr_r = scr[0:1, 0:512]
            P.wait("scalar", "pe")
            P.wait("scalar", "dv")
            P.op("scalar", lambda sc, m_r=m_r, sum_ps=sum_ps:
                 sc.activation(m_r, sum_ps, AF.Copy, scale=1.0 / D))
            P.drain("scalar")
            P.op("scalar", lambda sc, v_r=v_r, sq_ps=sq_ps:
                 sc.activation(v_r, sq_ps, AF.Copy, scale=1.0 / D), inc=("ac", 1))
            P.drain("scalar")
            P.wait("vector", "ac")
            t1 = scr[0:1, 0:512]
            P.op("vector", lambda v, t1=t1, m_r=m_r: v.tensor_mul(t1, m_r, m_r))
            P.drain("vector")
            P.op("vector", lambda v, t1=t1, v_r=v_r: v.tensor_sub(v_r, v_r, t1),
                 inc=("dv", 1))
            P.drain("vector")
            P.wait("scalar", "dv")
            P.op("scalar", lambda sc, v_r=v_r:
                 sc.activation(v_r, v_r, AF.Ln,
               bias=cf_sb[0:1, CC["eps_ln"]:CC["eps_ln"] + 1]))
            P.drain("scalar")
            P.op("scalar", lambda sc, v_r=v_r:
                 sc.activation(v_r, v_r, AF.Exp, scale=-0.5), inc=("ac", 1))
            P.drain("scalar")
            P.wait("vector", "ac")
            P.op("vector", lambda v, mr_r=mr_r, m_r=m_r, v_r=v_r:
                 v.tensor_mul(mr_r, m_r, v_r), inc=("dv", 1))
            P.drain("vector")
            rb = bank(2)
            mrb = bank(3)
            P.wait("tensor", "dv")
            P.op("tensor", lambda t, v_r=v_r, rb=rb:
                 t.matmul(rb, ONEST, v_r, start=True, stop=True))
            P.op("tensor", lambda t, mr_r=mr_r, mrb=mrb:
                 t.matmul(mrb, ONEST, mr_r, start=True, stop=True), inc=("pe", 1))
            P.wait("vector", "pe")
            for dc in range(NCD):
                u = scr[:, 1024 + (dc % 2) * 512: 1024 + (dc % 2) * 512 + 512]
                P.op("vector", lambda v, u=u, dc=dc, t0=t0, rb=rb:
                     v.tensor_mul(u, xv(dc, t0, 512), rb))
                P.drain("vector")
                P.op("vector", lambda v, u=u, dc=dc, t0=t0, mrb=mrb:
                     v.tensor_sub(xnv(dc, t0, 512), u, mrb), inc=("dv", 1))
                P.drain("vector")
        return P.val("dv")

    def emit_park(dep_dv):
        P.wait("sync", "dv", dep_dv)
        P.op("sync", lambda s: s.dma_start(out=park_h.ap(), in_=x_sb[:]),
             inc=("xi", 16))
        return P.val("xi")

    def emit_unpark(dep_dv):
        P.wait("sync", "dv", dep_dv)
        P.wait("sync", "pe")
        P.wait("sync", "ac")
        P.wait("sync", "xi")
        P.op("sync", lambda s: s.dma_start(out=x_sb[:], in_=park_h.ap()),
             inc=("xi", 16))
        return P.val("xi")

    def emit_attn_a(x_xd, x_dv):
        xn_ready = emit_ln(x_xd, x_dv)
        park_done = emit_park(xn_ready)

        P.wait("gpsimd", "xi", park_done)
        P.op("gpsimd", lambda g: g.dma_start(out=wres[:], in_=aw_h.ap()[0]),
             inc=("wa", 16))
        wq_ld = ("wa", P.val("wa"))

        def proj_fm(dst, wsbuf, wld):
            evac = {}
            n = 0
            for qc in range(NCD):
                for T in range(NT):
                    pb = bank(n % 2)
                    P.wait("tensor", wld[0], wld[1])
                    P.wait("tensor", "dv", xn_ready)
                    if n >= 2:
                        P.wait("tensor", "ac", evac[n - 2])
                    else:
                        P.wait("tensor", "ac")
                    for dc in range(NCD):
                        P.op("tensor", lambda t, qc=qc, T=T, dc=dc, pb=pb, w=wsbuf:
                             t.matmul(pb,
                                      w[:, dc * 1024 + qc * 128:
                                        dc * 1024 + qc * 128 + 128],
                                      xnv(dc, T * 512, 512),
                                      start=(dc == 0), stop=(dc == NCD - 1)),
                             inc=("pe", 1) if dc == NCD - 1 else None)
                    P.wait("scalar", "pe")
                    evac[n] = P.op("scalar", lambda sc, qc=qc, T=T, pb=pb, dst=dst:
                                   sc.activation(fmv(dst, qc, T * 512, 512), pb,
                                                 AF.Copy), inc=("ac", 1))
                    n += 1
            return P.val("ac")

        q_done = proj_fm(bufA, wres, wq_ld)
        P.op("gpsimd", lambda g: g.dma_start(out=wres2[:], in_=aw_h.ap()[1]),
             inc=("wb", 16))
        wk_ld = ("wb", P.val("wb"))
        k_done = proj_fm(bufB, wres2, wk_ld)
        P.wait("gpsimd", "ac", q_done)
        P.op("gpsimd", lambda g: g.dma_start(out=wres[:], in_=aw_h.ap()[2]),
             inc=("wa", 16))
        wv_ld = ("wa", P.val("wa"))

        P.op("vector", lambda v: v.memset(
            bufC[:].rearrange("p (k h c) -> p k h c", h=16, c=65)[:, :, :, 64],
            1.0), inc=("dv", 1))
        vdone = {}
        for tt in range(NKT):
            pb = ps[:, 1024 + (tt % 2) * 1024: 1024 + (tt % 2) * 1024 + 1024]
            P.wait("tensor", wv_ld[0], wv_ld[1])
            if tt >= 2:
                P.wait("tensor", "dv", vdone[tt - 2])
            for dc in range(NCD):
                for hf in range(2):
                    P.op("tensor", lambda t, tt=tt, dc=dc, hf=hf, pb=pb:
                         t.matmul(pb[:, hf * 512:hf * 512 + 512],
                                  xnv(dc, tt * 128, 128),
                                  wres[:, dc * 1024 + hf * 512:
                                       dc * 1024 + hf * 512 + 512],
                                  start=(dc == 0), stop=(dc == NCD - 1)),
                         inc=("pe", 1) if (dc == NCD - 1 and hf == 1) else None)
            P.wait("vector", "pe")
            vdone[tt] = P.op("vector", lambda v, tt=tt, pb=pb:
                 v.tensor_copy(
                     bufC[:].rearrange("p (k h c) -> p k h c", h=16, c=65)
                     [:, tt, :, 0:64],
                     pb.rearrange("p (h c) -> p h c", c=64)), inc=("dv", 1))
        v_done = P.val("dv")

        def norms(src, dst, src_ac):
            for T in range(NT):
                ssq = bank(0) if T % 2 == 0 else bank(1)
                sq_done = {}
                for ch in range(NCD):
                    s = scr[:, (ch % 2) * 512:(ch % 2) * 512 + 512]
                    P.wait("scalar", "ac", src_ac)
                    if ch >= 2:
                        P.wait("scalar", "pe", sq_done[ch - 2])
                    P.op("scalar", lambda sc, ch=ch, T=T, s=s, src=src:
                         sc.activation(s, fmv(src, ch, T * 512, 512), AF.Square),
                         inc=("ac", 1))
                    P.wait("tensor", "ac")
                    sq_done[ch] = P.op("tensor", lambda t, ch=ch, s=s, ssq=ssq:
                         t.matmul(ssq[0:16, :], m16(ch), s,
                                  start=(ch == 0), stop=(ch == NCD - 1)),
                         inc=("pe", 1))
                P.wait("scalar", "pe")
                d = dst[0:16, T * 512:T * 512 + 512]
                P.op("scalar", lambda sc, d=d, ssq=ssq:
                     sc.activation(d, ssq[0:16, :], AF.Ln,
               bias=cf_sb[0:16, CC["eps_q"]:CC["eps_q"] + 1]))
                P.drain("scalar")
                P.op("scalar", lambda sc, d=d:
                     sc.activation(d, d, AF.Exp, scale=0.5), inc=("ac", 1))
                P.drain("scalar")
            return P.val("ac")

        qn_done = norms(bufA, qn_sb, q_done)
        kn_done = norms(bufB, kn_sb, k_done)
        P.wait("vector", "ac", kn_done)
        P.op("vector", lambda v: v.reduce_max(mk_sb[0:16, 0:1], kn_sb[0:16, :],
                                              axis=AX.X))
        P.drain("vector")
        P.op("vector", lambda v: v.tensor_scalar(
            bb_sb[0:16, :], qn_sb[0:16, :], mk_sb[0:16, 0:1], None, ALU.mult),
            inc=("dv", 1))
        P.drain("vector")
        bb_done = P.val("dv")


        for h in range(H):
            ch, par = h // 2, h % 2
            for qt in range(NT):
                pb_av = bank(2 + (h * NT + qt) % 2)
                P.wait("tensor", "dv", bb_done)
                P.wait("tensor", "dv")
                P.wait("tensor", "ac")
                P.op("tensor", lambda t, h=h, qt=qt:
                     t.matmul(ps[0:1, 2560:3072],
                              cf_sb[0:16, CC["ident"] + h:CC["ident"] + h + 1],
                              bb_sb[0:16, qt * 512:qt * 512 + 512],
                              start=True, stop=True), inc=("pe", 1))
                P.wait("scalar", "pe")
                P.op("scalar", lambda sc: sc.activation(
                    trow[0:1, 0:512], ps[0:1, 2560:3072], AF.Copy), inc=("ac", 1))
                P.wait("tensor", "ac")
                P.op("tensor", lambda t: t.matmul(
                    bank(6), ONEST, trow[0:1, 0:512], start=True, stop=True),
                    inc=("pe", 1))
                P.wait("scalar", "pe")
                P.op("scalar", lambda sc: sc.activation(
                    scr[:, 0:512], bank(6), AF.Copy), inc=("ac", 1))
                bbc_ready = P.val("ac")
                sub_done, exp_done, av_done = {}, {}, {}
                for kt in range(NKT):
                    st = bank(kt % 2)
                    pslot = pb_sb[:, (kt % 3) * 512:(kt % 3) * 512 + 512]
                    P.wait("tensor", "ac", bbc_ready)
                    if kt >= 2:
                        P.wait("tensor", "dv", sub_done[kt - 2])
                    P.op("tensor", lambda t, ch=ch, par=par, kt=kt, qt=qt, st=st:
                         t.matmul(st,
                                  fmv(bufB, ch, kt * 128, 128)
                                  [par * 64:par * 64 + 64, :],
                                  fmv(bufA, ch, qt * 512, 512)
                                  [par * 64:par * 64 + 64, :],
                                  start=True, stop=True), inc=("pe", 1))
                    P.wait("vector", "pe")
                    e = scr[:, 512 + (kt % 2) * 512: 512 + (kt % 2) * 512 + 512]
                    if kt >= 2:
                        P.wait("vector", "ac", exp_done[kt - 2])
                    sub_done[kt] = P.op("vector", lambda v, e=e, st=st:
                         v.tensor_sub(e, st, scr[:, 0:512]), inc=("dv", 1))
                    P.wait("scalar", "dv")
                    if kt >= 3:
                        P.wait("scalar", "pe", av_done[kt - 3])
                    exp_done[kt] = P.op("scalar", lambda sc, e=e, pslot=pslot:
                         sc.activation(pslot, e, AF.Exp, scale=0.125),
                         inc=("ac", 1))
                    P.wait("tensor", "ac")
                    P.wait("tensor", "dv", v_done)
                    if par == 0:
                        av_done[kt] = P.op(
                            "tensor", lambda t, kt=kt, h=h, pslot=pslot, pb_av=pb_av:
                            t.matmul(pb_av[0:65, :],
                                     bufC[:, kt * 1040 + h * 65:
                                          kt * 1040 + h * 65 + 65],
                                     pslot, start=(kt == 0), stop=(kt == NKT - 1)),
                            inc=("pe", 1))
                    else:
                        P.op("tensor", lambda t, kt=kt, h=h, pslot=pslot, pb_av=pb_av:
                             t.matmul(pb_av[64:128, :],
                                      bufC[:, kt * 1040 + h * 65:
                                           kt * 1040 + h * 65 + 64],
                                      pslot, start=(kt == 0), stop=(kt == NKT - 1)))
                        av_done[kt] = P.op("tensor", lambda t, kt=kt, pslot=pslot:
                             t.matmul(ps[0:1, 2048:2560], ONESB, pslot,
                                      start=(kt == 0), stop=(kt == NKT - 1)),
                             inc=("pe", 1))
                P.wait("vector", "pe")
                if par == 0:
                    P.op("vector", lambda v, pb_av=pb_av:
                         v.reciprocal(trow[64:65, 0:512], pb_av[64:65, :]),
                         inc=("dv", 1))
                    P.wait("tensor", "dv")
                    P.op("tensor", lambda t: t.matmul(
                        ps[0:64, 2560:3072], ONEST64[:, 0:64], trow[64:65, 0:512],
                        start=True, stop=True), inc=("pe", 1))
                    P.wait("scalar", "pe")
                    P.op("scalar", lambda sc: sc.activation(
                        scr[0:64, 1536:2048], ps[0:64, 2560:3072], AF.Copy),
                        inc=("ac", 1))
                    P.wait("vector", "ac")
                    P.op("vector", lambda v, ch=ch, qt=qt, pb_av=pb_av:
                         v.tensor_mul(fmv(ofm_sb, ch, qt * 512, 512)[0:64, :],
                                      pb_av[0:64, :], scr[0:64, 1536:2048]),
                         inc=("dv", 1))
                else:
                    P.op("vector", lambda v:
                         v.reciprocal(trow[0:1, 0:512], ps[0:1, 2048:2560]),
                         inc=("dv", 1))
                    P.wait("tensor", "dv")
                    P.op("tensor", lambda t: t.matmul(
                        ps[64:128, 2560:3072], ONEST[:, 0:64], trow[0:1, 0:512],
                        start=True, stop=True), inc=("pe", 1))
                    P.wait("scalar", "pe")
                    P.op("scalar", lambda sc: sc.activation(
                        scr[64:128, 1536:2048], ps[64:128, 2560:3072], AF.Copy),
                        inc=("ac", 1))
                    P.wait("vector", "ac")
                    P.op("vector", lambda v, ch=ch, qt=qt, pb_av=pb_av:
                         v.tensor_mul(fmv(ofm_sb, ch, qt * 512, 512)[64:128, :],
                                      pb_av[64:128, :], scr[64:128, 1536:2048]),
                         inc=("dv", 1))
        o_done = P.val("dv")

        P.wait("gpsimd", "pe")
        P.op("gpsimd", lambda g: g.dma_start(out=wresC[:], in_=aw_h.ap()[3]),
             inc=("wx", 16))
        wo_ld = P.val("wx")
        unpark_done = emit_unpark(o_done)

        P.wait("tensor", "xi", unpark_done)
        P.wait("tensor", "wx", wo_ld)
        P.wait("tensor", "dv", o_done)
        prev = {}
        n = 0
        for dc in range(NCD):
            for T in range(NT):
                pb = bank(n % 2)
                if n >= 2:
                    P.wait("tensor", "dv", prev[n - 2])
                for fc in range(NCD):
                    P.op("tensor", lambda t, dc=dc, T=T, fc=fc, pb=pb:
                         t.matmul(pb,
                                  wresC[:, fc * 1024 + dc * 128:
                                        fc * 1024 + dc * 128 + 128],
                                  fmv(ofm_sb, fc, T * 512, 512),
                                  start=(fc == 0), stop=(fc == NCD - 1)),
                         inc=("pe", 1) if fc == NCD - 1 else None)
                P.wait("scalar", "pe")
                s = scr[:, (n % 2) * 512:(n % 2) * 512 + 512]
                P.op("scalar", lambda sc, dc=dc, pb=pb, s=s:
                     sc.activation(s, pb, AF.Identity, bias=bo_sb[:, dc:dc + 1]),
                     inc=("ac", 1))
                P.wait("vector", "ac")
                prev[n] = P.op("vector", lambda v, dc=dc, T=T, s=s:
                     v.tensor_add(xv(dc, T * 512, 512), xv(dc, T * 512, 512), s),
                     inc=("dv", 1))
                n += 1
        P.drain("vector")
        return P.val("dv")

    def emit_ffn(x_xd, x_dv, experts=tuple(range(8))):
        P.wait("gpsimd", "pe")
        P.wait("gpsimd", "dv")
        xn_ready = emit_ln(x_xd, x_dv)

        lg = ps[:, 2048:2048 + 128]
        P.wait("tensor", "cw", INIT_CW)
        P.wait("tensor", "dv", xn_ready)
        for tt in range(NKT):
            for dc in range(NCD):
                P.op("tensor", lambda t, tt=tt, dc=dc:
                     t.matmul(lg[:, tt * 8:tt * 8 + 8],
                              xnv(dc, tt * 128, 128),
                              wr_sb[:, dc * 8:dc * 8 + 8],
                              start=(tt == 0 and dc == 0),
                              stop=(tt == NKT - 1 and dc == NCD - 1)),
                     inc=(("pe", 1) if (tt == NKT - 1 and dc == NCD - 1)
                          else None))
        P.wait("vector", "pe")
        P.op("vector", lambda v: v.tensor_copy(scr[:, 0:128], lg))
        P.drain("vector")
        lgv = scr[:, 0:128].rearrange("p (t e) -> p t e", e=8)
        P.op("vector", lambda v, lgv=lgv:
             v.reduce_max(scr[:, 128:144], lgv, axis=AX.X))
        P.drain("vector")
        for tt in range(NKT):
            P.op("vector", lambda v, tt=tt: v.tensor_scalar(
                scr[:, 256 + tt * 8:256 + tt * 8 + 8],
                scr[:, tt * 8:tt * 8 + 8],
                scr[:, 128 + tt:128 + tt + 1], None, ALU.subtract))
        P.drain("vector")
        P.op("vector", lambda v: v.nop(), inc=("dv", 1))
        P.wait("scalar", "dv")
        P.op("scalar", lambda sc: sc.activation(
            scr[:, 384:512], scr[:, 256:384], AF.Exp), inc=("ac", 1))
        P.wait("vector", "ac")
        ev = scr[:, 384:512].rearrange("p (t e) -> p t e", e=8)
        P.op("vector", lambda v, ev=ev:
             v.reduce_sum(scr[:, 144:160], ev, axis=AX.X))
        P.drain("vector")
        P.op("vector", lambda v: v.reciprocal(scr[:, 160:176], scr[:, 144:160]))
        P.drain("vector")
        for tt in range(NKT):
            P.op("vector", lambda v, tt=tt: v.tensor_scalar(
                scr[:, 256 + tt * 8:256 + tt * 8 + 8],
                scr[:, tt * 8:tt * 8 + 8],
                scr[:, 128 + tt:128 + tt + 1], None, ALU.is_equal))
        P.drain("vector")
        for tt in range(NKT):
            P.op("vector", lambda v, tt=tt: v.tensor_scalar(
                scr[:, 256 + tt * 8:256 + tt * 8 + 8],
                scr[:, 256 + tt * 8:256 + tt * 8 + 8],
                scr[:, 160 + tt:160 + tt + 1], 2.0, ALU.mult, ALU.mult))
        P.drain("vector")
        P.op("vector", lambda v: v.nop(), inc=("dv", 1))
        P.wait("tensor", "dv")
        wt_done = {}
        for tt in range(NKT):
            wt = ps[0:8, 2560 + (tt % 2) * 512: 2560 + (tt % 2) * 512 + 128]
            if tt >= 2:
                P.wait("tensor", "dv", wt_done[tt - 2])
            P.op("tensor", lambda t, tt=tt, wt=wt:
                 t.transpose(wt, scr[:, 256 + tt * 8:256 + tt * 8 + 8], IDF),
                 inc=("pe", 1))
            P.wait("vector", "pe")
            wt_done[tt] = P.op("vector", lambda v, tt=tt, wt=wt:
                 v.tensor_copy(wrow_sb[0:8, tt * 128:tt * 128 + 128], wt),
                 inc=("dv", 1))
        wrow_done = P.val("dv")

        ug_keys = [(j, gT, hb) for j in experts for gT in range(2)
                   for hb in range(NHB)]
        d_keys = [(j, gT, mc) for j in experts for gT in range(2)
                  for mc in range(8)]
        ug_free = {}
        d_free = {}
        ug_emitted = [0]
        d_emitted = [0]

        def pump_ug(upto):
            while ug_emitted[0] < min(upto, len(ug_keys)):
                i = ug_emitted[0]
                j, gT, hb = ug_keys[i]
                sem = "f0" if i % 2 == 0 else "f1"
                if i >= 2 and (i - 2) in ug_free:
                    P.wait("gpsimd", "pe", ug_free[i - 2])
                P.op("gpsimd", lambda g, j=j, hb=hb, i=i:
                     g.dma_start(out=wffa[i % 2], in_=fwug_h.ap()[j, hb]),
                     inc=(sem, 16))
                ug_ld[i] = (sem, P.val(sem))
                ug_emitted[0] += 1

        def pump_d(upto):
            while d_emitted[0] < min(upto, len(d_keys)):
                i = d_emitted[0]
                j, gT, mc = d_keys[i]
                sem = "g0" if i % 2 == 0 else "g1"
                if i >= 2 and (i - 2) in d_free:
                    P.wait("gpsimd", "pe", d_free[i - 2])
                P.op("gpsimd", lambda g, j=j, mc=mc, i=i:
                     g.dma_start(out=wffd[i % 2], in_=fwd_h.ap()[j, mc]),
                     inc=(sem, 16))
                d_ld[i] = (sem, P.val(sem))
                d_emitted[0] += 1

        ug_ld, d_ld = {}, {}
        pump_ug(2)
        pump_d(1)

        wb_done = 0
        for j in experts:
            if j < 7:
                P.wait("tensor", "dv", wrow_done)
                for T in range(NT):
                    P.wait("tensor", "dv")
                    P.wait("tensor", "ac")
                    P.op("tensor", lambda t, j=j, T=T: t.matmul(
                        ps[0:1, 2048:2560], cb_sb[0:8, j:j + 1],
                        wrow_sb[0:8, T * 512:T * 512 + 512],
                        start=True, stop=True), inc=("pe", 1))
                    P.wait("scalar", "pe")
                    P.op("scalar", lambda sc: sc.activation(
                        trow[0:1, 0:512], ps[0:1, 2048:2560], AF.Copy),
                        inc=("ac", 1))
                    P.wait("tensor", "ac")
                    P.op("tensor", lambda t: t.matmul(
                        ps[:, 2560:3072], ONEST, trow[0:1, 0:512],
                        start=True, stop=True), inc=("pe", 1))
                    P.wait("vector", "pe")
                    P.op("vector", lambda v, T=T: v.tensor_copy(
                        wb_sb[:, T * 512:T * 512 + 512], ps[:, 2560:3072]),
                        inc=("dv", 1))
                wb_done = P.val("dv")

            for gT in range(2):
                g0 = gT * 1024
                act_done = {}
                down_prev_dv = P.val("dv")
                silu_done = {}
                for hb in range(NHB):
                    iu = (experts.index(j) * 2 + gT) * NHB + hb
                    pump_ug(iu + 2)
                    wt = wffa[iu % 2]
                    ub0 = bank(0) if hb % 2 == 0 else bank(4)
                    ub1 = bank(1) if hb % 2 == 0 else bank(5)
                    P.wait("tensor", ug_ld[iu][0], ug_ld[iu][1])
                    P.wait("tensor", "dv", xn_ready)
                    if hb >= 2:
                        P.wait("tensor", "dv", act_done[hb - 2])
                    else:
                        P.wait("tensor", "dv", down_prev_dv)
                    for dc in range(NCD):
                        P.op("tensor", lambda t, dc=dc, g0=g0, ub0=ub0, wt=wt:
                             t.matmul(ub0, wt[:, dc * 128:dc * 128 + 128],
                                      xnv(dc, g0, 512),
                                      start=(dc == 0), stop=(dc == NCD - 1)))
                        P.op("tensor", lambda t, dc=dc, g0=g0, ub1=ub1, wt=wt:
                             t.matmul(ub1, wt[:, dc * 128:dc * 128 + 128],
                                      xnv(dc, g0 + 512, 512),
                                      start=(dc == 0), stop=(dc == NCD - 1)))
                    if hb >= 1:
                        P.wait("tensor", "ac", silu_done[hb - 1])
                    for dc in range(NCD):
                        P.op("tensor", lambda t, dc=dc, g0=g0, wt=wt:
                             t.matmul(bank(2),
                                      wt[:, 1024 + dc * 128:1024 + dc * 128 + 128],
                                      xnv(dc, g0, 512),
                                      start=(dc == 0), stop=(dc == NCD - 1)))
                        P.op("tensor", lambda t, dc=dc, g0=g0, wt=wt:
                             t.matmul(bank(3),
                                      wt[:, 1024 + dc * 128:1024 + dc * 128 + 128],
                                      xnv(dc, g0 + 512, 512),
                                      start=(dc == 0), stop=(dc == NCD - 1)),
                             inc=("pe", 1) if dc == NCD - 1 else None)
                    ug_free[iu] = P.val("pe")
                    P.wait("scalar", "pe")
                    P.wait("scalar", "dv")
                    sg0 = scr[:, 1024:1536]
                    sg1 = scr[:, 1536:2048]
                    P.op("scalar", lambda sc, sg0=sg0: sc.activation(
                        sg0, bank(2), AF.Silu))
                    silu_done[hb] = P.op("scalar", lambda sc, sg1=sg1:
                         sc.activation(sg1, bank(3), AF.Silu), inc=("ac", 1))
                    P.wait("vector", "ac")
                    P.op("vector", lambda v, hb=hb, ub0=ub0, sg0=sg0:
                         v.tensor_mul(act_sb[:, hb * 1024:hb * 1024 + 512],
                                      ub0, sg0))
                    act_done[hb] = P.op("vector", lambda v, hb=hb, ub1=ub1, sg1=sg1:
                         v.tensor_mul(act_sb[:, hb * 1024 + 512:hb * 1024 + 1024],
                                      ub1, sg1), inc=("dv", 1))
                a_done = P.val("dv")

                devac = {}
                for mc in (range(8) if not FFN_SKIP_B[0] else []):
                    idd = (experts.index(j) * 2 + gT) * 8 + mc
                    pump_d(idd + 2)
                    wdt = wffd[idd % 2]
                    db0 = bank((mc % 2) * 2)
                    db1 = bank((mc % 2) * 2 + 1)
                    P.wait("tensor", d_ld[idd][0], d_ld[idd][1])
                    P.wait("tensor", "dv", a_done)
                    if mc >= 2:
                        P.wait("tensor", "dv", devac[mc - 2])
                    else:
                        P.wait("tensor", "dv")
                    for hb in range(NHB):
                        P.op("tensor", lambda t, hb=hb, db0=db0, wdt=wdt:
                             t.matmul(db0, wdt[:, hb * 128:hb * 128 + 128],
                                      act_sb[:, hb * 1024:hb * 1024 + 512],
                                      start=(hb == 0), stop=(hb == NHB - 1)))
                        P.op("tensor", lambda t, hb=hb, db1=db1, wdt=wdt:
                             t.matmul(db1, wdt[:, hb * 128:hb * 128 + 128],
                                      act_sb[:, hb * 1024 + 512:hb * 1024 + 1024],
                                      start=(hb == 0), stop=(hb == NHB - 1)),
                             inc=("pe", 1) if hb == NHB - 1 else None)
                    d_free[idd] = P.val("pe")
                    P.wait("vector", "pe")
                    if j < 7:
                        P.wait("vector", "dv", wb_done)
                        P.op("vector", lambda v, db0=db0, g0=g0:
                             v.tensor_mul(scr[:, 0:512], db0,
                                          wb_sb[:, g0:g0 + 512]))
                        P.op("vector", lambda v, db1=db1, g0=g0:
                             v.tensor_mul(scr[:, 512:1024], db1,
                                          wb_sb[:, g0 + 512:g0 + 1024]))
                        P.drain("vector")
                        P.op("vector", lambda v, mc=mc, g0=g0:
                             v.tensor_add(xv(mc, g0, 512), xv(mc, g0, 512),
                                          scr[:, 0:512]))
                        devac[mc] = P.op("vector", lambda v, mc=mc, g0=g0:
                             v.tensor_add(xv(mc, g0 + 512, 512),
                                          xv(mc, g0 + 512, 512),
                                          scr[:, 512:1024]), inc=("dv", 1))
                        P.drain("vector")
                    else:
                        P.op("vector", lambda v, mc=mc, g0=g0, db0=db0:
                             v.tensor_add(xv(mc, g0, 512), xv(mc, g0, 512), db0))
                        devac[mc] = P.op("vector", lambda v, mc=mc, g0=g0, db1=db1:
                             v.tensor_add(xv(mc, g0 + 512, 512),
                                          xv(mc, g0 + 512, 512), db1),
                             inc=("dv", 1))
                        P.drain("vector")
        return P.val("dv")

    def emit_titans(x_xd, x_dv):
        xn_ready = emit_ln(x_xd, x_dv)
        park_done = emit_park(xn_ready)
        P.wait("gpsimd", "xi", park_done)

        def kboff(st, h):
            return (st // 2) * 1024 + h * 64

        def tit_proj(widx, kind, dst):
            P.wait("gpsimd", "pe")
            P.op("gpsimd", lambda g, widx=widx:
                 g.dma_start(out=wres[:], in_=bw_h.ap()[widx]), inc=("wa", 16))
            wld = P.val("wa")
            pidx = widx if widx < 3 else 0
            for qc in range(NCD):
                P.wait("tensor", "wa", wld)
                P.wait("tensor", "dv", xn_ready)
                P.wait("tensor", "dv")
                P.wait("tensor", "ac")
                for T in range(NT):
                    for dc in range(NCD):
                        P.op("tensor", lambda t, qc=qc, T=T, dc=dc:
                             t.matmul(ps[:, T * 512:T * 512 + 512],
                                      wres[:, dc * 1024 + qc * 128:
                                            dc * 1024 + qc * 128 + 128],
                                      xnv(dc, T * 512, 512),
                                      start=(dc == 0), stop=(dc == NCD - 1)),
                             inc=(("pe", 1) if (T == NT - 1 and dc == NCD - 1)
                                  else None))
                if kind == "g":
                    P.wait("scalar", "pe")
                    for T in range(NT):
                        P.op("scalar", lambda sc, qc=qc, T=T, dst=dst:
                             sc.activation(fmv(dst, qc, T * 512, 512),
                                           ps[:, T * 512:T * 512 + 512],
                                           AF.Copy), inc=("ac", 1))
                    continue
                P.wait("vector", "pe")
                w3 = w3p_sb[:, pidx * 8 + qc:pidx * 8 + qc + 1]
                P.op("vector", lambda v, w3=w3: v.tensor_scalar(
                    scr[:, 0:2048], ps[:, 0:2048], w3, None, ALU.mult))
                P.drain("vector")
                for k in (1, 2, 3):
                    wk = conv_sb[:, pidx * 32 + qc * 4 + (3 - k):
                                 pidx * 32 + qc * 4 + (3 - k) + 1]
                    P.op("vector", lambda v, k=k, wk=wk: v.scalar_tensor_tensor(
                        scr[:, k:2048], ps[:, 0:2048 - k], wk, scr[:, k:2048],
                        ALU.mult, ALU.add))
                    P.drain("vector")
                P.op("vector", lambda v: v.nop(), inc=("dv", 1))
                P.wait("scalar", "dv")
                P.op("scalar", lambda sc: sc.activation(
                    ps[:, 0:2048], scr[:, 0:2048], AF.Silu), inc=("ac", 1))
                P.drain("scalar")
                if kind == "v":
                    P.wait("vector", "ac")
                    P.op("vector", lambda v: v.tensor_copy(scr[:, 0:2048],
                                                           ps[:, 0:2048]),
                         inc=("dv", 1))
                    P.drain("vector")
                    P.wait("tensor", "dv")
                    tr_done = {}
                    for b in range(NKT):
                        trp = ps[:, 2048 + (b % 2) * 512:
                                 2048 + (b % 2) * 512 + 128]
                        if b >= 2:
                            P.wait("tensor", "dv", tr_done[b - 2])
                        P.op("tensor", lambda t, b=b, trp=trp:
                             t.transpose(trp, scr[:, b * 128:b * 128 + 128], IDF),
                             inc=("pe", 1))
                        P.wait("vector", "pe")
                        kb = (qc * 2048 + b * 128) // 128
                        blk, = (kb % 16,)
                        tr_done[b] = P.op(
                            "vector", lambda v, qc=qc, blk=blk, trp=trp:
                            v.tensor_copy(
                                vtm_t[:, blk * 1024 + 2 * qc * 64:
                                      blk * 1024 + 2 * qc * 64 + 128],
                                trp), inc=("dv", 1))
                    continue
                for T in range(NT):
                    P.wait("scalar", "ac")
                    P.wait("scalar", "dv")
                    P.op("scalar", lambda sc, T=T: sc.activation(
                        scr[:, 0:512], ps[:, T * 512:T * 512 + 512], AF.Square),
                        inc=("ac", 1))
                    P.wait("tensor", "ac")
                    P.op("tensor", lambda t: t.matmul(
                        ps[0:2, 2048:2560], MASK2, scr[:, 0:512],
                        start=True, stop=True), inc=("pe", 1))
                    P.wait("scalar", "pe")
                    P.op("scalar", lambda sc: sc.activation(
                        trow[0:2, 0:512], ps[0:2, 2048:2560], AF.Ln,
        bias=cf_sb[0:2, CC["eps_l2"]:CC["eps_l2"] + 1]))
                    P.drain("scalar")
                    P.op("scalar", lambda sc: sc.activation(
                        trow[0:2, 0:512], trow[0:2, 0:512], AF.Exp, scale=-0.5),
                        inc=("ac", 1))
                    P.drain("scalar")
                    P.wait("tensor", "ac")
                    P.op("tensor", lambda t: t.matmul(
                        ps[:, 2560:3072], SEL2T, trow[0:2, 0:512],
                        start=True, stop=True), inc=("pe", 1))
                    P.wait("scalar", "pe")
                    P.op("scalar", lambda sc: sc.activation(
                        scr[:, 512:1024], ps[:, 2560:3072], AF.Copy),
                        inc=("ac", 1))
                    P.wait("vector", "ac")
                    P.op("vector", lambda v, qc=qc, T=T, dst=dst:
                         v.tensor_mul(fmv(dst, qc, T * 512, 512),
                                      ps[:, T * 512:T * 512 + 512],
                                      scr[:, 512:1024]), inc=("dv", 1))
                    P.drain("vector")

        tit_proj(0, "qk", bufA)
        tit_proj(1, "qk", bufB)
        tit_proj(2, "v", None)
        tit_proj(3, "g", gate_sb)
        projs_dv = P.val("dv")
        projs_ac = P.val("ac")

        P.wait("tensor", "dv", projs_dv)
        P.wait("tensor", "ac", projs_ac)
        ktr_done = {}
        for qc in range(NCD):
            for b in range(NKT):
                i = qc * NKT + b
                trp = psb[:, (i % 2) * 512:(i % 2) * 512 + 128]
                if i >= 1:
                    P.wait("tensor", "dv", ktr_done[i - 1])
                P.op("tensor", lambda t, qc=qc, b=b, trp=trp:
                     t.transpose(trp, fmv(bufB, qc, b * 128, 128), IDB),
                     inc=("pe", 1))
                P.wait("vector", "pe")
                ktr_done[i] = P.op("vector", lambda v, qc=qc, b=b, trp=trp:
                     v.tensor_copy(
                         ktm_sb[:, b * 1024 + 2 * qc * 64:
                                b * 1024 + 2 * qc * 64 + 128], trp),
                     inc=("dv", 1))
        ktm_done = P.val("dv")

        if TIT_ONLY[0] == "gen":
            unpark_done = emit_unpark(ktm_done)
            P.wait("vector", "xi", unpark_done)
            P.op("vector", lambda v: v.nop(), inc=("dv", 1))
            return P.val("dv")

        P.wait("sync", "dv", projs_dv)
        P.wait("sync", "ac", projs_ac)
        P.op("sync", lambda s: s.dma_start(out=w0_sb[:], in_=w0_h.ap()),
             inc=("xi", 16))
        w0_ld = P.val("xi")
        P.wait("tensor", "xi", w0_ld)
        P.wait("tensor", "dv", ktm_done)
        P.op("tensor", lambda t: t.matmul(
            ps[:, 0:512], IDF, w0_sb[:], start=True, stop=True), inc=("pe", 1))
        P.wait("scalar", "pe")
        P.op("scalar", lambda sc: sc.activation(wsb[:], ps[:, 0:512], AF.Copy),
             inc=("ac", 1))
        wsb_ready = P.val("ac")

        for st in range(TIT_STEPS[0] if TIT_STEPS[0] is not None else NSUP):
            tb = 64 * (st % 2)
            t0 = st * 64
            P.wait("tensor", "dv")
            head_order = [hh for hh in range(H) if hh % 2 == 0] +                          [hh for hh in range(H) if hh % 2 == 1]
            for hi, h in enumerate(head_order):
                ch, par = h // 2, h % 2
                qb = 512 + par * 512 + (h // 2) * 64
                P.op("tensor", lambda t, ch=ch, par=par, qb=qb, tb=tb, t0=t0, h=h,
                     hi=hi:
                     t.matmul(ps[tb:tb + 64, qb:qb + 64],
                              fmv(bufB, ch, t0, 64)[par * 64:par * 64 + 64, :],
                              fmv(bufA, ch, t0, 64)[par * 64:par * 64 + 64, :],
                              start=(hi % 8 == 0), stop=(hi % 8 == 7),
                              skip_group_check=True),
                     inc=("pe", 1) if hi == H - 1 else None)
            P.wait("vector", "pe")
            if TIT_CUT[0] == "mmonly":
                P.op("vector", lambda v: v.nop(), inc=("dv", 1))
            for hh in (range(8) if TIT_CUT[0] != "mmonly" else []):
                P.op("vector", lambda v, tb=tb, hh=hh: v.tensor_mul(
                    qkt_sb[tb:tb + 64, hh * 64:hh * 64 + 64],
                    ps[tb:tb + 64, 512 + hh * 64:512 + hh * 64 + 64],
                    BMASK8[tb:tb + 64, :]))
            for hh in (range(8) if TIT_CUT[0] != "mmonly" else []):
                P.op("vector", lambda v, tb=tb, hh=hh: v.tensor_mul(
                    qkt_sb[tb:tb + 64, 512 + hh * 64:512 + hh * 64 + 64],
                    ps[tb:tb + 64, 1024 + hh * 64:1024 + hh * 64 + 64],
                    BMASK8[tb:tb + 64, :]),
                    inc=(("dv", 1) if hh == 7 else None))
            P.drain("vector")
            qkt_dv = P.val("dv")
            if TIT_CUT[0] == "qkt":
                continue
            if TIT_CUT[0] == "mmonly":
                continue

            P.wait("tensor", "dv", qkt_dv)
            P.wait("tensor", "ac", wsb_ready)

            def o_slice(h):
                par = h % 2
                base = 1536 if par == 0 else 3072
                return ps[par * 64:par * 64 + 64,
                          base + (h // 2) * 64:base + (h // 2) * 64 + 64]

            for h in range(H):
                ch, par = h // 2, h % 2
                P.op("tensor", lambda t, h=h, ch=ch, par=par, t0=t0:
                     t.matmul(o_slice(h),
                              wsb[par * 64:par * 64 + 64,
                                  (h // 2) * 64:(h // 2) * 64 + 64],
                              fmv(bufA, ch, t0, 64)[par * 64:par * 64 + 64, :],
                              start=(h <= 1), stop=False, skip_group_check=True),
                     inc=("pe", 1) if h == H - 1 else None)
            P.wait("tensor", "pe")
            for h in range(H):
                par = h % 2
                qcol = par * 512 + (h // 2) * 64
                P.op("tensor", lambda t, h=h, par=par, tb=tb, st=st, qcol=qcol:
                     t.matmul(o_slice(h),
                              vtm_t[tb:tb + 64, kboff(st, h):kboff(st, h) + 64],
                              qkt_sb[tb:tb + 64, qcol:qcol + 64],
                              start=False, stop=(h >= H - 2),
                              skip_group_check=True),
                     inc=("pe", 1) if h == H - 1 else None)
            o_pe = P.val("pe")
            for h in range(H):
                par = h % 2
                P.op("tensor", lambda t, h=h, par=par, tb=tb, st=st:
                     t.matmul(ps[par * 64:par * 64 + 64,
                                 (h // 2) * 64:(h // 2) * 64 + 64],
                              ktm_sb[tb:tb + 64, kboff(st, h):kboff(st, h) + 64],
                              vtm_t[tb:tb + 64, kboff(st, h):kboff(st, h) + 64],
                              start=False, stop=(h == H - 1),
                              skip_group_check=True),
                     inc=("pe", 1) if h == H - 1 else None)
            wupd_pe = P.val("pe")
            P.wait("scalar", "pe", wupd_pe)
            P.op("scalar", lambda sc: sc.activation(wsb[:], ps[:, 0:512], AF.Copy),
                 inc=("ac", 1))
            wsb_ready = P.val("ac")
            if TIT_CUT[0] == "w":
                P.wait("vector", "pe", o_pe)
                P.op("vector", lambda v: v.tensor_copy(scr[:, 0:512],
                                                       ps[:, 1536:2048]),
                     inc=("dv", 1))
                P.drain("vector")
                continue

            P.wait("vector", "pe", o_pe)
            P.op("vector", lambda v: v.tensor_copy(scr[0:64, 0:512],
                                                   ps[0:64, 1536:2048]))
            P.op("vector", lambda v: v.tensor_copy(scr[64:128, 0:512],
                                                   ps[64:128, 3072:3584]),
                 inc=("dv", 1))
            P.drain("vector")
            P.wait("tensor", "dv")
            P.wait("tensor", "ac")
            P.op("tensor", lambda t: t.matmul(
                ps[0:2, 2048:2560], MASK2, scr[:, 0:512], start=True, stop=True),
                inc=("pe", 1))
            P.wait("scalar", "pe")
            P.op("scalar", lambda sc: sc.activation(
                trow[0:2, 0:512], ps[0:2, 2048:2560], AF.Copy, scale=1.0 / 64),
                inc=("ac", 1))
            P.wait("tensor", "ac")
            P.wait("tensor", "dv")
            P.op("tensor", lambda t: t.matmul(
                ps[:, 2560:3072], SEL2T, trow[0:2, 0:512], start=True, stop=True),
                inc=("pe", 1))
            P.wait("vector", "pe")
            P.op("vector", lambda v: v.tensor_sub(scr[:, 512:1024], scr[:, 0:512],
                                                  ps[:, 2560:3072]),
                 inc=("dv", 1))
            P.drain("vector")
            P.wait("scalar", "dv")
            P.op("scalar", lambda sc: sc.activation(
                scr[:, 1024:1536], scr[:, 512:1024], AF.Square), inc=("ac", 1))
            P.wait("tensor", "ac")
            P.op("tensor", lambda t: t.matmul(
                ps[0:2, 2048:2560], MASK2, scr[:, 1024:1536],
                start=True, stop=True), inc=("pe", 1))
            P.wait("scalar", "pe")
            P.op("scalar", lambda sc: sc.activation(
                scr[0:2, 0:512], ps[0:2, 2048:2560], AF.Ln,
        bias=cf_sb[0:2, CC["eps_gn"]:CC["eps_gn"] + 1]))
            P.drain("scalar")
            P.op("scalar", lambda sc: sc.activation(
                trow[0:2, 0:512], scr[0:2, 0:512], AF.Exp, scale=-0.5,
                bias=cf_sb[0:2, CC["ln8"]:CC["ln8"] + 1]), inc=("ac", 1))
            P.drain("scalar")
            P.wait("tensor", "ac")
            P.op("tensor", lambda t: t.matmul(
                ps[:, 2560:3072], SEL2T, trow[0:2, 0:512],
                start=True, stop=True), inc=("pe", 1))
            P.wait("vector", "pe")
            P.op("vector", lambda v: v.tensor_mul(
                scr[:, 512:1024], scr[:, 512:1024], ps[:, 2560:3072]))
            P.drain("vector")
            P.op("vector", lambda v: v.tensor_scalar(
                scr[:, 512:1024], scr[:, 512:1024], gb_sb[:, 0:1], gb_sb[:, 1:2],
                ALU.mult, ALU.add))
            P.drain("vector")
            P.op("vector", lambda v, t0=t0: v.tensor_mul(
                ofm_sb[:].rearrange("p (c t) -> p c t", t=L)[:, :, t0:t0 + 64],
                scr[:, 512:1024].rearrange("p (c t) -> p c t", t=64),
                gate_sb[:].rearrange("p (c t) -> p c t", t=L)[:, :, t0:t0 + 64]),
                inc=("dv", 1))
            P.drain("vector")
        og_done = P.val("dv")

        if TIT_ONLY[0] == "scan":
            unpark_done = emit_unpark(og_done)
            P.wait("vector", "xi", unpark_done)
            P.op("vector", lambda v: v.nop(), inc=("dv", 1))
            return P.val("dv")

        P.wait("gpsimd", "pe")
        P.op("gpsimd", lambda g: g.dma_start(out=wresA[:], in_=bw_h.ap()[4]),
             inc=("wx", 16))
        wo_ld = P.val("wx")
        unpark_done = emit_unpark(og_done)

        P.wait("tensor", "xi", unpark_done)
        P.wait("tensor", "wx", wo_ld)
        P.wait("tensor", "dv", og_done)
        prev = {}
        n = 0
        for dc in range(NCD):
            for T in range(NT):
                pb = bank(n % 2)
                if n >= 2:
                    P.wait("tensor", "dv", prev[n - 2])
                for fc in range(NCD):
                    P.op("tensor", lambda t, dc=dc, T=T, fc=fc, pb=pb:
                         t.matmul(pb,
                                  wresA[:, fc * 1024 + dc * 128:
                                        fc * 1024 + dc * 128 + 128],
                                  fmv(ofm_sb, fc, T * 512, 512),
                                  start=(fc == 0), stop=(fc == NCD - 1)),
                         inc=("pe", 1) if fc == NCD - 1 else None)
                P.wait("vector", "pe")
                prev[n] = P.op("vector", lambda v, dc=dc, T=T, pb=pb:
                     v.tensor_add(xv(dc, T * 512, 512), xv(dc, T * 512, 512), pb),
                     inc=("dv", 1))
                n += 1
        P.drain("vector")
        return P.val("dv")

    for s in range(2):
        if s == 1:
            P.wait("sync", "out", 16)
            P.wait("sync", "dv")
            if DEBUG_BARRIER[0]:
                for e in ("tensor", "scalar", "vector", "gpsimd"):
                    P.wait(e, "out", 16)
                    P.wait(e, "dv")
                    P.wait(e, "ac")
                    P.wait(e, "pe")
        P.wait("sync", "xi")
        P.op("sync", lambda sy, s=s: sy.dma_start(out=x_sb[:], in_=x_h.ap()[s]),
             inc=("xi", 16))
        x_xd = P.val("xi")
        x_dv = P.val("dv")
        for ph in PHASES:
            if ph == "attn_a":
                x_dv = emit_attn_a(x_xd, x_dv)
                x_xd = P.val("xi")
            elif ph in ("ffn0", "ffn1"):
                x_dv = emit_ffn(x_xd, x_dv)
                x_xd = P.val("xi")
            elif ph == "ffns":
                x_dv = emit_ffn(x_xd, x_dv, experts=(7,))
                x_xd = P.val("xi")
            elif ph == "ffn1e":
                x_dv = emit_ffn(x_xd, x_dv, experts=(0, 7))
                x_xd = P.val("xi")
            elif ph == "ffnr":
                x_dv = emit_ffn(x_xd, x_dv, experts=())
                x_xd = P.val("xi")
            elif ph == "titans":
                x_dv = emit_titans(x_xd, x_dv)
                x_xd = P.val("xi")
            elif ph == "lncheck":
                xn_ready = emit_ln(x_xd, x_dv)
                P.wait("vector", "dv", xn_ready)
                for dc in range(NCD):
                    P.op("vector", lambda v, dc=dc: v.tensor_copy(
                        xv(dc, 0, L), xnv(dc, 0, L)), inc=("dv", 1))
                P.drain("vector")
                x_dv = P.val("dv")
        P.wait("sync", "dv", x_dv)
        P.op("sync", lambda sy, s=s: sy.dma_start(out=y_h.ap()[s], in_=x_sb[:]),
             inc=("out", 16))
    P.wait("sync", "out", P.val("out"))
    if DEBUG_DUMP[0]:
        P.wait("gpsimd", "out", P.val("out"))
        P.wait("gpsimd", "dv")
        P.wait("gpsimd", "ac")
        P.wait("gpsimd", "pe")
        which = DEBUG_DUMP[0]
        if which == "xn_act":
            P.op("gpsimd", lambda g: g.dma_start(
                out=y_h.ap()[0].bitcast(BF16)[:, 0:16384], in_=xn_sb[:]),
                 inc=("out", 16))
            P.op("gpsimd", lambda g: g.dma_start(
                out=y_h.ap()[1].bitcast(BF16)[:, 0:16384],
                in_=act_sb[:, 0:16384]), inc=("out", 16))
        elif which == "rows":
            P.op("gpsimd", lambda g: g.dma_start(out=y_h.ap()[0][:, 0:1024],
                                                 in_=trow[:]), inc=("out", 16))
            P.op("gpsimd", lambda g: g.dma_start(out=y_h.ap()[0][:, 1024:3072],
                                                 in_=scr[:]), inc=("out", 16))
        elif which == "wff":
            P.op("gpsimd", lambda g: g.dma_start(out=y_h.ap()[0][:, 0:4096],
                                                 in_=wffa_t[:]), inc=("out", 16))
            P.op("gpsimd", lambda g: g.dma_start(out=y_h.ap()[0][:, 4096:12288],
                                                 in_=wffd_t[:]), inc=("out", 16))
        P.wait("sync", "out", P.val("out"))

    import contextlib
    with contextlib.ExitStack() as ctx:
        sems = {n: ctx.enter_context(nc.semaphore("s_" + n)) for n in sem_names}
        with nc.Block() as block:
            block.sync(lambda e: P.replay("sync", e, sems))
            block.tensor(lambda e: P.replay("tensor", e, sems))
            block.vector(lambda e: P.replay("vector", e, sems))
            block.scalar(lambda e: P.replay("scalar", e, sems))
            block.gpsimd(lambda e: P.replay("gpsimd", e, sems))
    return nc


def kernel(**inputs):
    g = stage_inputs(inputs)
    nc = build_program()
    shared = {k: g[k] for k in ("aw", "bw", "fwug", "fwd", "wr", "bo", "convw",
                                "w0", "gb", "constf", "constb")}
    in_maps = []
    for i in range(8):
        m = dict(shared)
        m["x"] = g["x_cores"][i]
        in_maps.append(m)
    res = run_bass_kernel_spmd(nc, in_maps, core_ids=list(range(8)))
    if DEBUG_DUMP[0]:
        return [r["y"] for r in res.results]
    out = np.empty((16, L, D), dtype=np.float32)
    for i in range(8):
        y = res.results[i]["y"]
        y = y.reshape(2, 128, 8, L).transpose(0, 2, 1, 3).reshape(2, D, L)
        out[2 * i:2 * i + 2] = y.transpose(0, 2, 1)
    return out
